# version 08 (final)
# baseline (speedup 1.0000x reference)
import contextlib
import numpy as np
import ml_dtypes
import concourse.bass as bass
import concourse.mybir as mybir
from concourse.bass_utils import run_bass_kernel_spmd

F32 = mybir.dt.float32
BF16 = mybir.dt.bfloat16
AF = mybir.ActivationFunctionType
ALU = mybir.AluOpType
AX = mybir.AxisListType

NCORES = 8
D = 2048
DK = D // 128
HID = 5504
HC = HID // 128
HID_REAL = 5461
NTOK = 2048
SEQ = 4096
FTC = 1024
FTT = FTC // 128
ATC = 512
ATT = ATC // 128
GT = 16
NG = ATC // GT
H = 16
MYH = 8
EPS = 1e-5
CNT_REAL = float(HID_REAL * D)
SEGS = [8, 8, 8, 8, 8, 3]
assert sum(SEGS) == HC

ENGS = ("sync", "gpsimd", "tensor", "vector", "scalar")
_PHASES = ("ffn1", "attn", "ffn2")


class Plan:

    def __init__(self):
        self.ops = {e: [] for e in ENGS}
        self.counts = {}
        self.waited = {e: {} for e in ENGS}

    def sem(self, name):
        self.counts.setdefault(name, 0)

    def op(self, eng, fn, waits=(), inc=None, inc_by=1):
        waits = list(waits)
        if inc is not None and inc_by == 16:
            waits.append((inc, self.counts[inc]))
        pruned = []
        w = self.waited[eng]
        for s, v in waits:
            if v > w.get(s, 0):
                w[s] = v
                pruned.append((s, v))
        if inc is not None:
            self.counts[inc] += inc_by
        self.ops[eng].append((fn, tuple(pruned), inc, inc_by))
        return self.counts[inc] if inc is not None else None

    def cur(self, name):
        return self.counts[name]

    def emit(self, eng_obj, eng_name, sems):
        for fn, waits, inc, inc_by in self.ops[eng_name]:
            for s, v in waits:
                eng_obj.wait_ge(sems[s], v)
            r = fn(eng_obj)
            if inc is not None and r is not None:
                r.then_inc(sems[inc], inc_by)


def build_nc(mode="fused"):
    nc = bass.Bass("TRN2", target_bir_lowering=False, debug=False)

    d_x = nc.dram_tensor("x_in", [SEQ, D], F32, kind="ExternalInput").ap()
    d_y = nc.dram_tensor("y_out", [NTOK, D], F32, kind="ExternalOutput").ap()
    d_wq = nc.dram_tensor("wqTm", [D, MYH * 128], F32, kind="ExternalInput").ap()
    d_wk = nc.dram_tensor("wkT", [D, D], F32, kind="ExternalInput").ap()
    d_wv = nc.dram_tensor("wvT", [D, D], F32, kind="ExternalInput").ap()
    d_wo = nc.dram_tensor("woT", [D, D], F32, kind="ExternalInput").ap()
    d_w1 = nc.dram_tensor("w1T", [D, HID], F32, kind="ExternalInput").ap()
    d_w3 = nc.dram_tensor("w3T", [D, HID], F32, kind="ExternalInput").ap()
    d_w2 = nc.dram_tensor("w2T", [HID, D], F32, kind="ExternalInput").ap()
    d_bqkv = nc.dram_tensor("bqkv", [128, 40], F32, kind="ExternalInput").ap()
    d_bo = nc.dram_tensor("bo_row", [1, D], BF16, kind="ExternalInput").ap()
    d_ones1b = nc.dram_tensor("ones1b", [1, 128], BF16, kind="ExternalInput").ap()
    d_ident = nc.dram_tensor("ident", [128, 128], BF16, kind="ExternalInput").ap()
    d_mask = nc.dram_tensor("maskB", [128, 256], F32, kind="ExternalInput").ap()
    d_cst = nc.dram_tensor("cst", [128, 4], F32, kind="ExternalInput").ap()
    d_ones1 = nc.dram_tensor("ones1", [1, 128], F32, kind="ExternalInput").ap()
    d_ones128 = nc.dram_tensor("ones128", [128, 1], F32, kind="ExternalInput").ap()
    d_xm1 = nc.dram_tensor("xm1", [SEQ, D], F32, kind="Internal").ap()
    d_xm2 = nc.dram_tensor("xm2", [NTOK, D], F32, kind="Internal").ap()
    d_oloc = nc.dram_tensor("oloc", [MYH, 128, SEQ], BF16, kind="Internal").ap()

    d_q1c = nc.dram_tensor("q1c", [HC, 128, D], BF16, kind="Internal").ap()
    d_pjc = nc.dram_tensor("pjc", [14, 128, DK * 512], BF16, kind="Internal").ap()
    d_q3c = nc.dram_tensor("q3c", [HC, 128, D], BF16, kind="Internal").ap()
    d_q2c = nc.dram_tensor("q2c", [HC, 128, D], BF16, kind="Internal").ap()

    v_w1 = d_w1.rearrange("(c p) f -> p c f", p=128)
    v_w3 = d_w3.rearrange("(c p) f -> p c f", p=128)
    v_wqm = d_wq.rearrange("(c p) f -> p c f", p=128)
    v_wk = d_wk.rearrange("(c p) f -> p c f", p=128)
    v_wv = d_wv.rearrange("(c p) f -> p c f", p=128)
    v_wo = d_wo.rearrange("(c p) f -> p c f", p=128)
    v_x = d_x.rearrange("(t p) d -> p t d", p=128)
    v_xm1f = d_xm1.rearrange("(t p) d -> p t d", p=128)
    v_oloc = d_oloc.rearrange("h d s -> d h s")
    v_y = d_y.rearrange("(t p) d -> p t d", p=128)
    v_xm2 = d_xm2.rearrange("(t p) d -> p t d", p=128)
    fws = [d_w1.flatten().rearrange("(n p f) -> n p f", p=128, f=2048),
           d_w3.flatten().rearrange("(n p f) -> n p f", p=128, f=2048),
           d_w2.flatten().rearrange("(n p f) -> n p f", p=128, f=2048)]

    p = Plan()
    for s in ("s_x", "s_wa", "s_wb", "s_gp", "s_gp2", "s_pe", "s_ve", "s_se",
              "s_out", "s_fl", "s_cst"):
        p.sem(s)
    OP = p.op

    es = contextlib.ExitStack()
    with es:
        sbt = lambda name, shape, dt: es.enter_context(nc.sbuf_tensor(name, shape, dt))
        xch = sbt("xch", [128, FTT * D], F32)
        xln = sbt("xln", [128, DK * FTC], BF16)
        big = sbt("big", [128, 24576], BF16)
        oarn = sbt("oarn", [128, 8192], BF16)
        warn = sbt("warn", [128, 16384], BF16)
        abuf = sbt("abuf", [128, 4096], BF16)
        bo_row = sbt("bo_rows", [1, D], BF16)
        ones1b = sbt("ones1bs", [1, 128], BF16)
        stats = sbt("stats", [128, 64], F32)
        sbc = sbt("sbc", [128, 8], F32)
        prt = sbt("prt", [128, 144], F32)
        pr1 = sbt("pr1", [128, 4], F32)
        sv3 = sbt("sv3", [1, 16], F32)
        bqkv = sbt("bqkvs", [128, 40], F32)
        cst = sbt("csts", [128, 4], F32)
        ident = sbt("idents", [128, 128], BF16)
        maskB = sbt("maskBs", [128, 256], F32)
        ones1 = sbt("ones1s", [1, 128], F32)
        ones128 = sbt("ones128s", [128, 1], F32)
        nbuf = sbt("nbuf", [1, 32], mybir.dt.int32)
        fbuf = sbt("fbuf", [1, 32], mybir.dt.int32)
        asc8 = abuf[:, 0:2048].bitcast(F32)
        a8 = abuf[:, 2048:3072]
        at_sb = abuf[:, 3072:3328]
        vt_sb = abuf[:, 3328:3584]

        PS = [es.enter_context(nc.psum_tensor(f"ps{i}", [128, 512], F32)) for i in range(8)]
        sems = {s: es.enter_context(nc.semaphore(s)) for s in
                ("s_x", "s_wa", "s_wb", "s_gp", "s_gp2", "s_pe", "s_ve", "s_se",
                 "s_out", "s_fl", "s_cst")}
        block = es.enter_context(nc.Block())

        psum_free = {i: ("s_ve", 0) for i in range(8)}
        state = {"lntmp": ("s_ve", 0)}
        lntmp_f = warn[:, 12288:14336]
        scr_f = warn[:, 14336:16384].bitcast(F32)
        lntmp_a = xln[:, 12288:14336]
        scr_a = xln[:, 14336:16384].bitcast(F32)

        for dst, src in ((bqkv, d_bqkv), (cst, d_cst), (ident, d_ident),
                         (maskB, d_mask), (ones1, d_ones1), (ones128, d_ones128),
                         (bo_row, d_bo), (ones1b, d_ones1b)):
            OP("sync", lambda e, d=dst, s=src: e.dma_start(out=d[:, :], in_=s[:, :]),
               inc="s_x", inc_by=16)
        preload_done = p.cur("s_x")

        slabs = [oarn[:, 0:4096].bitcast(F32), oarn[:, 4096:8192].bitcast(F32)]
        consumed = {}
        ci = 0
        for wi in range(3):
            for j in range(43):
                par = ci % 2
                psem = "s_wa" if par == 0 else "s_wb"
                waits = [consumed[par]] if par in consumed else []
                OP("sync", lambda e, s=slabs[par], src=fws[wi][j]: e.dma_start(out=s, in_=src),
                   waits=waits, inc=psem, inc_by=16)
                ld = p.cur(psem)
                OP("vector", lambda e, d=prt[:, wi * 48 + j: wi * 48 + j + 1], s=slabs[par]:
                   e.tensor_reduce(d, s, AX.X, ALU.add, apply_absolute_value=True),
                   waits=[(psem, ld)], inc="s_ve")
                consumed[par] = ("s_ve", p.cur("s_ve"))
                ci += 1
        OP("vector", lambda e: e.drain())
        for wi in range(3):
            OP("vector", lambda e, w=wi: e.tensor_reduce(
                pr1[:, w:w + 1], prt[:, w * 48:w * 48 + 43], AX.X, ALU.add),
               inc="s_ve" if wi == 2 else None)
        ve_pr = p.cur("s_ve")
        OP("tensor", lambda e: e.matmul(PS[0][0:1, 0:3], ones128[:, 0:1], pr1[:, 0:3],
                                        start=True, stop=True),
           waits=[("s_ve", ve_pr), ("s_x", preload_done)], inc="s_pe")
        pe_s = p.cur("s_pe")
        OP("vector", lambda e: e.tensor_scalar(sv3[0:1, 0:3], PS[0][0:1, 0:3],
                                               1.0 / CNT_REAL, None, ALU.mult),
           waits=[("s_pe", pe_s)], inc="s_ve")
        psum_free[0] = ("s_ve", p.cur("s_ve"))
        der = sv3[0:1, 8:16]
        OP("vector", lambda e: e.drain())
        OP("vector", lambda e: e.tensor_scalar(der[:, 0:1], sv3[0:1, 0:1], 0.5, None, ALU.mult))
        OP("vector", lambda e: e.tensor_tensor(der[:, 1:2], sv3[0:1, 2:3], sv3[0:1, 1:2], ALU.mult))
        specs = [(0, 0.5), (0, -0.5), (1, 0.5), (1, -0.5), (2, 0.5), (2, -0.5)]
        for k, (src_i, sc) in enumerate(specs):
            OP("vector", lambda e, k=k, si=src_i, sc=sc: e.tensor_scalar(
                der[:, 2 + k:3 + k], sv3[0:1, si:si + 1], sc, None, ALU.mult))
        OP("vector", lambda e: e.drain())
        OP("vector", lambda e: e.tensor_scalar(der[:, 1:2], der[:, 1:2], 0.125, None, ALU.mult))
        OP("vector", lambda e: e.drain())
        OP("vector", lambda e: e.nop(), inc="s_ve")
        ve_der = p.cur("s_ve")
        OP("tensor", lambda e: e.matmul(PS[1][:, 0:8], ones1[0:1, :], der,
                                        start=True, stop=True),
           waits=[("s_ve", ve_der)], inc="s_pe")
        pe_b = p.cur("s_pe")
        OP("vector", lambda e: e.tensor_copy(sbc[:], PS[1][:, 0:8]),
           waits=[("s_pe", pe_b)], inc="s_ve")
        ve_sbc = p.cur("s_ve")
        psum_free[1] = ("s_ve", ve_sbc)

        S1V = sbc[:, 0:1]
        GAM = sbc[:, 1:2]
        THR = {("w1", 0): sbc[:, 2:3], ("w1", 1): sbc[:, 3:4],
               ("w3", 0): sbc[:, 4:5], ("w3", 1): sbc[:, 5:6],
               ("w2", 0): sbc[:, 6:7], ("w2", 1): sbc[:, 7:8]}

        def emit_ln(ntt, tcw, extra_waits, lntmp, scr, half_waits=None):
            nm = stats[:, 0:ntt]
            sqa = stats[:, 16:16 + ntt]
            sqb = stats[:, 24:24 + ntt]
            rstd = stats[:, 32:32 + ntt]
            nmr = stats[:, 40:40 + ntt]
            scr2 = lntmp[:, 0:2048].bitcast(F32)
            lnbufs = [lntmp, scr.bitcast(BF16)]
            xlnv = xln[:, 0:DK * tcw].rearrange("p (c t) -> p c t", c=DK)
            for tt in range(ntt):
                xt = xch[:, tt * D:(tt + 1) * D]
                rw = []
                if tt == 0:
                    rw = extra_waits
                elif half_waits is not None and tt == ntt // 2:
                    rw = half_waits
                OP("vector", lambda e, d=nm[:, tt:tt + 1], s=xt: e.tensor_reduce(
                    d, s, AX.X, ALU.add, negate=True),
                   waits=rw, inc="s_ve")
                if tt == 0:
                    sq_w = extra_waits + [state.get("scr", ("s_ve", 0)),
                                          state["lntmp"], state.get("lntmp2", ("s_ve", 0))]
                else:
                    sq_w = [("s_se", p.cur("s_se"))]
                    if half_waits is not None and tt == ntt // 2:
                        sq_w = sq_w + half_waits
                OP("scalar", lambda e, d=sqa[:, tt:tt + 1], s=xt: e.activation(
                    scr[:, 0:1024], s[:, 0:1024], AF.Square, accum_out=d), waits=sq_w)
                OP("scalar", lambda e, d=sqb[:, tt:tt + 1], s=xt: e.activation(
                    scr2[:, 0:1024], s[:, 1024:2048], AF.Square, accum_out=d), inc="s_se")
            OP("scalar", lambda e: e.drain())
            OP("scalar", lambda e: e.nop(), inc="s_se")
            OP("vector", lambda e: e.drain())
            ve0, se0 = p.cur("s_ve"), p.cur("s_se")
            OP("vector", lambda e: e.tensor_scalar(nm, nm, 1.0 / D, None, ALU.mult),
               waits=[("s_se", se0)])
            OP("vector", lambda e: e.tensor_tensor(sqa, sqa, sqb, ALU.add))
            OP("vector", lambda e: e.drain())
            OP("vector", lambda e: e.tensor_tensor(sqb, nm, nm, ALU.mult))
            OP("vector", lambda e: e.tensor_scalar(sqa, sqa, 1.0 / D, None, ALU.mult))
            OP("vector", lambda e: e.drain())
            OP("vector", lambda e: e.tensor_tensor(sqa, sqa, sqb, ALU.subtract))
            OP("vector", lambda e: e.drain())
            OP("vector", lambda e: e.nop(), inc="s_ve")
            ve1 = p.cur("s_ve")
            OP("scalar", lambda e: e.activation(rstd, sqa, AF.Ln, bias=cst[:, 0:1]),
               waits=[("s_ve", ve1)], inc="s_se")
            OP("scalar", lambda e: e.activation(rstd, rstd, AF.Exp, scale=cst[:, 1:2]),
               waits=[("s_se", p.cur("s_se"))], inc="s_se")
            se1 = p.cur("s_se")
            OP("vector", lambda e: e.tensor_tensor(nmr, nm, rstd, ALU.mult),
               waits=[("s_se", se1)], inc="s_ve")
            ve2 = p.cur("s_ve")
            bmark = [state["lntmp"], state.get("lntmp2", ("s_ve", 0))]
            for tt in range(ntt):
                xt = xch[:, tt * D:(tt + 1) * D]
                buf = lnbufs[tt % 2]
                OP("scalar", lambda e, s=xt, tt=tt, b=buf: e.activation(
                    b[:, 0:D], s, AF.Identity, bias=nmr[:, tt:tt + 1], scale=rstd[:, tt:tt + 1]),
                   waits=[("s_ve", ve2), bmark[tt % 2]], inc="s_se")
                se_t = p.cur("s_se")
                for dk4 in range(4):
                    bi = 6 + (tt * 4 + dk4) % 2
                    bview = PS[bi][:, :].bitcast(BF16)
                    for j in range(4):
                        dk = dk4 * 4 + j
                        OP("tensor", lambda e, bv=bview, j=j, dk=dk, b=buf: e.transpose(
                            bv[:, j * 128:(j + 1) * 128],
                            b[:, dk * 128:(dk + 1) * 128], ident[:]),
                           waits=[("s_se", se_t), psum_free[bi]] if j == 0 else [],
                           inc="s_pe" if j == 3 else None)
                    pe_t = p.cur("s_pe")
                    dst = xlnv[:, dk4 * 4:dk4 * 4 + 4, tt * 128:(tt + 1) * 128]
                    OP("vector", lambda e, d=dst, bv=bview: e.tensor_copy(
                        d, bv[:, 0:512].rearrange("p (c t) -> p c t", c=4)),
                       waits=[("s_pe", pe_t)], inc="s_ve")
                    psum_free[bi] = ("s_ve", p.cur("s_ve"))
                bmark[tt % 2] = ("s_ve", p.cur("s_ve"))
            state["lntmp"] = bmark[0]
            state["lntmp2"] = bmark[1]
            state["scr"] = ("s_pe", p.cur("s_pe"))
            return ("s_ve", p.cur("s_ve"))

        cast_jobs = ([(f4, v_wqm, f4 * 512) for f4 in range(2)] +
                     [(2 + f4, v_wk, f4 * 512) for f4 in range(4)] +
                     [(6 + f4, v_wv, f4 * 512) for f4 in range(4)] +
                     [(10 + dc, v_wo, dc * 512) for dc in range(4)])

        def emit_ffn(src_view, dst_view, do_quant, entry_waits, ntc=2):
            is_ffn1 = do_quant
            q1slab = [warn[:, 0:2048], warn[:, 2048:4096]]
            q3slab = [warn[:, 4096:6144], warn[:, 6144:8192]]
            gseg = big[:, 0:8192]
            q2f = big[:, 8192:24576]
            f32slab = [oarn[:, 0:4096].bitcast(F32), oarn[:, 4096:8192].bitcast(F32)]
            acmp = [abuf[:, 0:2048], abuf[:, 2048:4096],
                    warn[:, 8192:10240], warn[:, 10240:12288]]
            sf = dict(state.get("ffn_sf", {}))

            FH = FTT // 2
            for tc in range(ntc):
                lw = [("s_out", p.cur("s_out")), state.get("st_gp", ("s_gp", 0))] + \
                     (entry_waits if tc == 0 else [])
                OP("sync", lambda e, s=src_view[:, tc * FTT:tc * FTT + FH, :]: e.dma_start(
                    out=xch[:, 0:FH * D].rearrange("p (t d) -> p t d", t=FH), in_=s),
                   waits=lw, inc="s_x", inc_by=16)
                xld = p.cur("s_x")
                OP("gpsimd", lambda e, s=src_view[:, tc * FTT + FH:(tc + 1) * FTT, :]:
                   e.dma_start(out=xch[:, FH * D:FTT * D].rearrange(
                       "p (t d) -> p t d", t=FH), in_=s),
                   waits=lw, inc="s_gp", inc_by=16)
                xg = p.cur("s_gp")
                xln_ready = emit_ln(FTT, FTC, [("s_x", xld)], lntmp_f, scr_f,
                                    half_waits=[("s_gp", xg)])

                hc0 = 0
                for seg, nq in enumerate(SEGS):
                    if is_ffn1 and not do_quant and cast_jobs:
                        cidx, cwv, coff = cast_jobs.pop(0)
                        OP("gpsimd", lambda e,
                           d=d_pjc[cidx].rearrange("p (c f) -> p c f", c=DK),
                           s=cwv[:, :, coff:coff + 512]: e.dma_start(out=d, in_=s),
                           inc="s_cst", inc_by=16)
                    def emit_q2_prefetch(nq=nq, hc0=hc0):
                        OP("sync", lambda e, d=q2f[:, 0:nq * 2048].rearrange(
                            "p (n f) -> p n f", n=nq),
                           c=d_q2c[hc0:hc0 + nq].rearrange("n p f -> p n f"):
                           e.dma_start(out=d, in_=c),
                           waits=[sf.get("q2h", ("s_pe", 0)),
                                  ("s_gp2", p.cur("s_gp2")) if do_quant else
                                  state.get("cache2", ("s_gp2", 0))],
                           inc="s_wa", inc_by=16)
                        return p.cur("s_wa")
                    if not do_quant:
                        q2ld = emit_q2_prefetch()
                    for hq in range(nq):
                        hc = hc0 + hq
                        par = hc % 2
                        pb = [(hc % 2) * 4 + i for i in range(4)]
                        if do_quant:
                            OP("sync", lambda e, src=v_w1[:, :, hc * 128:(hc + 1) * 128], d=f32slab[0]:
                               e.dma_start(out=d.rearrange("p (c f) -> p c f", c=DK), in_=src),
                               waits=[sf.get(("f32", 0), ("s_se", 0))], inc="s_wa", inc_by=16)
                            ld1 = p.cur("s_wa")
                            OP("sync", lambda e, src=v_w3[:, :, hc * 128:(hc + 1) * 128], d=f32slab[1]:
                               e.dma_start(out=d.rearrange("p (c f) -> p c f", c=DK), in_=src),
                               waits=[sf.get(("f32", 1), ("s_se", 0))], inc="s_wb", inc_by=16)
                            ld3 = p.cur("s_wb")
                            OP("scalar", lambda e: e.activation(
                                acmp[0], f32slab[0], AF.Sign, bias=THR[("w1", 1)]),
                               waits=[("s_wa", ld1), ("s_ve", ve_sbc),
                                      sf.get("ac01", ("s_ve", 0))])
                            OP("scalar", lambda e: e.activation(
                                acmp[1], f32slab[0], AF.Sign, bias=THR[("w1", 0)]),
                               inc="s_se")
                            se1 = p.cur("s_se")
                            OP("scalar", lambda e: e.activation(
                                acmp[2], f32slab[1], AF.Sign, bias=THR[("w3", 1)]),
                               waits=[("s_wb", ld3), sf.get("ac23", ("s_ve", 0))])
                            OP("scalar", lambda e: e.activation(
                                acmp[3], f32slab[1], AF.Sign, bias=THR[("w3", 0)]),
                               inc="s_se")
                            se13 = p.cur("s_se")
                            OP("vector", lambda e, q=q1slab[par]: e.tensor_tensor(
                                q, acmp[0], acmp[1], ALU.add),
                               waits=[("s_se", se13),
                                      sf.get(("w1q", par), ("s_pe", 0)),
                                      sf.get(("w3q", par), ("s_pe", 0)),
                                      ("s_gp", p.cur("s_gp"))])
                            OP("vector", lambda e, q=q3slab[par]: e.tensor_tensor(
                                q, acmp[2], acmp[3], ALU.add), inc="s_ve")
                            qve = p.cur("s_ve")
                            sf["ac23"] = ("s_ve", qve)
                            OP("gpsimd", lambda e, q=q1slab[par], c=d_q1c[hc]: e.dma_start(out=c, in_=q),
                               waits=[("s_ve", qve)], inc="s_gp", inc_by=16)
                            OP("gpsimd", lambda e, q=q3slab[par], c=d_q3c[hc]: e.dma_start(out=c, in_=q),
                               waits=[("s_ve", qve)], inc="s_gp", inc_by=16)
                            qready = [("s_ve", qve)]
                            OP("sync", lambda e, src=d_w2[hc * 128:(hc + 1) * 128, :], d=f32slab[0]:
                               e.dma_start(out=d, in_=src),
                               waits=[("s_se", se1)], inc="s_wb", inc_by=16)
                            ld2 = p.cur("s_wb")
                            OP("scalar", lambda e: e.activation(
                                acmp[0], f32slab[0], AF.Sign, bias=THR[("w2", 1)]),
                               waits=[("s_wb", ld2), ("s_ve", qve)])
                            OP("scalar", lambda e: e.activation(
                                acmp[1], f32slab[0], AF.Sign, bias=THR[("w2", 0)]),
                               inc="s_se")
                            se2 = p.cur("s_se")
                            sf[("f32", 0)] = ("s_se", se2)
                            sf[("f32", 1)] = ("s_se", se13)
                            OP("vector", lambda e, d=q2f[:, hq * 2048:(hq + 1) * 2048]:
                               e.tensor_tensor(d, acmp[0], acmp[1], ALU.add),
                               waits=[("s_se", se2), ("s_gp2", p.cur("s_gp2"))] +
                                     ([sf.get("q2h", ("s_pe", 0))] if hq == 0 else []),
                               inc="s_ve")
                            qve2 = p.cur("s_ve")
                            sf["ac01"] = ("s_ve", qve2)
                            OP("gpsimd", lambda e, q=q2f[:, hq * 2048:(hq + 1) * 2048],
                               c=d_q2c[hc]: e.dma_start(out=c, in_=q),
                               waits=[("s_ve", qve2)], inc="s_gp2", inc_by=16)
                            q2rdy = ("s_ve", qve2)
                        else:
                            if par == 0:
                                n2 = min(2, hc0 + nq - hc)
                                gate = entry_waits if (tc == 0 and hc == 0) else []
                                OP("gpsimd", lambda e, d=warn[:, 0:n2 * 2048].rearrange(
                                    "p (n f) -> p n f", n=n2),
                                   c=d_q1c[hc:hc + n2].rearrange("n p f -> p n f"):
                                   e.dma_start(out=d, in_=c),
                                   waits=gate + [sf.get(("w1q", 0), ("s_pe", 0)),
                                                 sf.get(("w1q", 1), ("s_pe", 0)),
                                                 state.get("cache", ("s_gp", 0)),
                                                 state.get("cache2", ("s_gp2", 0))],
                                   inc="s_gp", inc_by=16)
                                OP("gpsimd", lambda e, d=warn[:, 4096:4096 + n2 * 2048].rearrange(
                                    "p (n f) -> p n f", n=n2),
                                   c=d_q3c[hc:hc + n2].rearrange("n p f -> p n f"):
                                   e.dma_start(out=d, in_=c),
                                   waits=[sf.get(("w3q", 0), ("s_pe", 0)),
                                          sf.get(("w3q", 1), ("s_pe", 0))],
                                   inc="s_gp", inc_by=16)
                                state["qld"] = ("s_gp", p.cur("s_gp"))
                            qready = [state["qld"]]
                        mmw = qready + [xln_ready] + [psum_free[b] for b in pb]
                        for dk in range(DK):
                            for wsl, b0 in ((q1slab[par], 0), (q3slab[par], 2)):
                                for half in range(2):
                                    OP("tensor", lambda e, l=wsl[:, dk * 128:(dk + 1) * 128],
                                       r=xln[:, dk * FTC + half * 512: dk * FTC + (half + 1) * 512],
                                       o=PS[pb[b0 + half]][:, :], s=(dk == 0), st=(dk == DK - 1):
                                       e.matmul(o, l, r, start=s, stop=st),
                                       waits=mmw if (dk == 0 and b0 == 0 and half == 0) else [],
                                       inc="s_pe" if (dk == DK - 1 and b0 == 2 and half == 1) else None)
                        pe_h = p.cur("s_pe")
                        sf[("w1q", par)] = ("s_pe", pe_h)
                        sf[("w3q", par)] = ("s_pe", pe_h)
                        for half in range(2):
                            OP("scalar", lambda e, src=PS[pb[half]][:, :],
                               d=scr_f[:, half * 512:(half + 1) * 512]:
                               e.activation(d, src, AF.Silu, scale=S1V),
                               waits=[("s_pe", pe_h), sf.get(("scr", half), ("s_ve", 0)),
                                      state.get("scr", ("s_se", 0))], inc="s_se")
                            se_h = p.cur("s_se")
                            OP("vector", lambda e, d=gseg[:, hq * FTC + half * 512: hq * FTC + (half + 1) * 512],
                               sl=scr_f[:, half * 512:(half + 1) * 512], h3=PS[pb[2 + half]][:, :]:
                               e.scalar_tensor_tensor(d, sl, GAM, h3, ALU.mult, ALU.mult),
                               waits=[("s_se", se_h)], inc="s_ve")
                            sf[("scr", half)] = ("s_ve", p.cur("s_ve"))
                        ve_g = p.cur("s_ve")
                        for b in pb:
                            psum_free[b] = ("s_ve", ve_g)
                    state["scr"] = ("s_ve", 0)
                    for tt in range(FTT):
                        for dc in range(4):
                            b = (tt % 2) * 4 + dc
                            w = ([] if do_quant else [("s_wa", q2ld)]) + \
                                [("s_ve", ve_g), psum_free[b]]
                            for hq in range(nq):
                                OP("tensor", lambda e,
                                   l=gseg[:, hq * FTC + tt * 128: hq * FTC + tt * 128 + 128],
                                   r=q2f[:, hq * 2048 + dc * 512: hq * 2048 + (dc + 1) * 512],
                                   o=PS[b][:, :], s=(hq == 0), st=(hq == nq - 1):
                                   e.matmul(o, l, r, start=s, stop=st),
                                   waits=w if hq == 0 else [],
                                   inc="s_pe" if hq == nq - 1 else None)
                            pe_y = p.cur("s_pe")
                            OP("vector", lambda e,
                               xp=xch[:, tt * D + dc * 512: tt * D + (dc + 1) * 512],
                               ps_=PS[b][:, :]: e.tensor_tensor(xp, xp, ps_, ALU.add),
                               waits=[("s_pe", pe_y)], inc="s_ve")
                            psum_free[b] = ("s_ve", p.cur("s_ve"))
                    sf["q2h"] = ("s_pe", p.cur("s_pe"))
                    hc0 += nq
                ve_all = p.cur("s_ve")
                OP("sync", lambda e, d=dst_view[:, tc * FTT:tc * FTT + FH, :]: e.dma_start(
                    out=d, in_=xch[:, 0:FH * D].rearrange("p (t d) -> p t d", t=FH)),
                   waits=[("s_ve", ve_all)], inc="s_out", inc_by=16)
                OP("gpsimd", lambda e, d=dst_view[:, tc * FTT + FH:(tc + 1) * FTT, :]:
                   e.dma_start(out=d, in_=xch[:, FH * D:FTT * D].rearrange(
                       "p (t d) -> p t d", t=FH)),
                   waits=[("s_ve", ve_all)], inc="s_gp", inc_by=16)
                state["st_gp"] = ("s_gp", p.cur("s_gp"))
                if do_quant and tc == 0:
                    state["cache"] = ("s_gp", p.cur("s_gp"))
                    state["cache2"] = ("s_gp2", p.cur("s_gp2"))
                    state["cache2q"] = ("s_gp2", p.cur("s_gp2"))
                    do_quant = False
            state["ffn_sf"] = {}

        def emit_attn_src(src_view, entry_waits):
            qpk = big[:, 0:4096]
            kpk = big[:, 4096:12288]
            vpk = big[:, 12288:20480]
            vt_sb2 = big[:, 20480:21504]
            wslab4 = [warn[:, 0:8192], warn[:, 8192:16384]]
            sf = {}
            asc4 = abuf[:, 0:2048].bitcast(F32)
            a4 = abuf[:, 2048:3072]
            at_sb2 = abuf[:, 3072:3584]
            mx4 = stats[:, 48:52]
            sm4 = stats[:, 52:56]
            projs = [(0, MYH, qpk, 0, 0, 8), (1, H, kpk, 2, 8, 16), (2, H, vpk, 6, 24, 16)]
            cst_done = ("s_cst", p.cur("s_cst"))
            AH = ATT // 2
            for tc in range(SEQ // ATC):
                lw = [state.get("axf", ("s_ve", 0)), state.get("st_gp", ("s_gp", 0))] + \
                     (entry_waits + [("s_out", p.cur("s_out"))] if tc == 0 else [])
                OP("sync", lambda e, s=src_view[:, tc * ATT:tc * ATT + AH, :]: e.dma_start(
                    out=xch[:, 0:AH * D].rearrange("p (t d) -> p t d", t=AH), in_=s),
                   waits=lw, inc="s_x", inc_by=16)
                xld = p.cur("s_x")
                OP("gpsimd", lambda e, s=src_view[:, tc * ATT + AH:(tc + 1) * ATT, :]:
                   e.dma_start(out=xch[:, AH * D:ATT * D].rearrange(
                       "p (t d) -> p t d", t=AH), in_=s),
                   waits=lw, inc="s_gp", inc_by=16)
                xg = p.cur("s_gp")
                xln_ready = emit_ln(ATT, ATC, [("s_x", xld)], lntmp_a, scr_a,
                                    half_waits=[("s_gp", xg)])
                state["axf"] = xln_ready
                ti = 0
                for pi, ntile, pack, base, boff, tgrp in projs:
                    for f4 in range(ntile // 4):
                        par = ti % 2
                        qeng, psem = (("gpsimd", "s_gp") if par == 0 else
                                      ("sync", "s_wb"))
                        OP(qeng, lambda e, s=d_pjc[base + f4], d=wslab4[par]:
                           e.dma_start(out=d, in_=s),
                           waits=[sf.get(("pw", par), ("s_pe", 0))] +
                                 ([cst_done] + entry_waits if (tc == 0 and ti < 2) else []),
                           inc=psem, inc_by=16)
                        ldw = p.cur(psem)
                        wsv = wslab4[par].rearrange("p (c f) -> p c f", c=DK)
                        for fi in range(4):
                            f = f4 * 4 + fi
                            bi = (ti * 4 + fi) % 4
                            mw = [(psem, ldw), xln_ready, psum_free[bi]]
                            for dk in range(DK):
                                OP("tensor", lambda e, l=wsv[:, dk, fi * 128:(fi + 1) * 128],
                                   r=xln[:, dk * ATC:dk * ATC + ATC],
                                   o=PS[bi][:, :], s=(dk == 0), st=(dk == DK - 1):
                                   e.matmul(o, l, r, start=s, stop=st),
                                   waits=mw if dk == 0 else [], inc="s_pe" if dk == DK - 1 else None)
                            pe_p = p.cur("s_pe")
                            if fi == 3:
                                sf[("pw", par)] = ("s_pe", pe_p)
                            dstv = pack.rearrange("p (g f t) -> p g f t", g=NG, f=tgrp)[:, :, f, :]
                            srcv = PS[bi][:, :].rearrange("p (g t) -> p g t", g=NG)
                            OP("scalar", lambda e, d=dstv, s=srcv,
                               b=bqkv[:, boff + f: boff + f + 1]:
                               e.activation(d, s, AF.Identity, bias=b),
                               waits=[("s_pe", pe_p)], inc="s_se")
                            psum_free[bi] = ("s_se", p.cur("s_se"))
                        ti += 1
                se_packs = p.cur("s_se")
                for b in range(NG // 4):
                    OP("vector", lambda e: e.drain())
                    for i in range(4):
                        g = b * 4 + i
                        par = g % 2
                        OP("tensor", lambda e, l=qpk[:, g * 128:(g + 1) * 128],
                           r=kpk[:, g * 256:(g + 1) * 256], o=PS[par][:, 0:256]:
                           e.matmul(o, l, r, start=True, stop=True),
                           waits=[("s_se", se_packs), psum_free[par]], inc="s_pe")
                        pe_g = p.cur("s_pe")
                        w = [("s_pe", pe_g)]
                        if b > 0 and i == 0:
                            w.append(state["a8free"])
                        OP("vector", lambda e, i=i, s=PS[par][:, 0:256]: e.scalar_tensor_tensor(
                            asc4[:, i * 256:(i + 1) * 256], s, cst[:, 2:3], maskB[:],
                            ALU.mult, ALU.add), waits=w, inc="s_ve")
                        psum_free[par] = ("s_ve", p.cur("s_ve"))
                    OP("vector", lambda e: e.drain())
                    for i in range(4):
                        OP("vector", lambda e, i=i: e.tensor_reduce(
                            mx4[:, i:i + 1], asc4[:, i * 256:(i + 1) * 256], AX.X, ALU.max,
                            negate=True), inc="s_ve" if i == 3 else None)
                    ve_rm = p.cur("s_ve")
                    for i in range(4):
                        OP("scalar", lambda e, i=i: e.activation(
                            asc4[:, i * 256:(i + 1) * 256], asc4[:, i * 256:(i + 1) * 256],
                            AF.Exp, bias=mx4[:, i:i + 1]),
                           waits=[("s_ve", ve_rm)] if i == 0 else [],
                           inc="s_se" if i == 3 else None)
                    se_exp = p.cur("s_se")
                    for i in range(4):
                        OP("vector", lambda e, i=i: e.tensor_reduce(
                            sm4[:, i:i + 1], asc4[:, i * 256:(i + 1) * 256], AX.X, ALU.add),
                           waits=[("s_se", se_exp)] if i == 0 else [])
                    OP("vector", lambda e: e.drain())
                    OP("vector", lambda e: e.reciprocal(sm4, sm4))
                    OP("vector", lambda e: e.drain())
                    for i in range(4):
                        w = []
                        if i == 0 and "at_free" in state:
                            w = [state["at_free"]]
                        OP("vector", lambda e, i=i: e.tensor_scalar(
                            a4[:, i * 256:(i + 1) * 256], asc4[:, i * 256:(i + 1) * 256],
                            sm4[:, i:i + 1], None, ALU.mult),
                           waits=w, inc="s_ve" if i == 3 else None)
                    ve_norm = p.cur("s_ve")
                    for i in range(4):
                        g = b * 4 + i
                        par = g % 2
                        for half in range(2):
                            bb = (4 + par) if half == 0 else (6 + par)
                            OP("tensor", lambda e, o=PS[bb][:, :].bitcast(BF16)[:, 0:128],
                               i_=a4[:, i * 256 + half * 128: i * 256 + (half + 1) * 128]:
                               e.transpose(o, i_, ident[:]),
                               waits=([("s_ve", ve_norm)] if half == 0 else []) + [psum_free[bb]])
                            OP("tensor", lambda e, o=PS[bb][:, :].bitcast(BF16)[:, 128:256],
                               i_=vpk[:, g * 256 + half * 128: g * 256 + (half + 1) * 128]:
                               e.transpose(o, i_, ident[:]),
                               inc="s_pe" if half == 1 else None)
                        pe_t = p.cur("s_pe")
                        for half in range(2):
                            bb = (4 + par) if half == 0 else (6 + par)
                            OP("vector", lambda e, d=at_sb2[:, par * 256 + half * 128: par * 256 + (half + 1) * 128],
                               s=PS[bb][:, :].bitcast(BF16)[:, 0:128]: e.tensor_copy(d, s),
                               waits=[("s_pe", pe_t)] if half == 0 else [])
                            OP("vector", lambda e, d=vt_sb2[:, par * 256 + half * 128: par * 256 + (half + 1) * 128],
                               s=PS[bb][:, :].bitcast(BF16)[:, 128:256]: e.tensor_copy(d, s),
                               inc="s_ve" if half == 1 else None)
                        ve_t = p.cur("s_ve")
                        psum_free[4 + par] = ("s_ve", ve_t)
                        psum_free[6 + par] = ("s_ve", ve_t)
                        for half in range(2):
                            OP("tensor", lambda e, l=vt_sb2[:, par * 256 + half * 128: par * 256 + (half + 1) * 128],
                               r=at_sb2[:, par * 256 + half * 128: par * 256 + (half + 1) * 128],
                               o=PS[2 + par][:, 0:128], s=(half == 0), st=(half == 1):
                               e.matmul(o, l, r, start=s, stop=st),
                               waits=[("s_ve", ve_t), psum_free[2 + par]] if half == 0 else [],
                               inc="s_pe" if half == 1 else None)
                        pe_av = p.cur("s_pe")
                        if i == 3:
                            state["a8free"] = ("s_pe", pe_av)
                            state["at_free"] = ("s_pe", pe_av)
                        dsto = oarn[:, 0:MYH * ATC].rearrange("p (f t) -> p f t", f=MYH)[
                            :, :, g * GT:(g + 1) * GT]
                        OP("scalar", lambda e, d=dsto,
                           s=PS[2 + par][:, 0:128].rearrange("p (f t) -> p f t", f=MYH):
                           e.activation(d, s, AF.Identity),
                           waits=[("s_pe", pe_av), state.pop("o_stored", ("s_out", 0))],
                           inc="s_se")
                        psum_free[2 + par] = ("s_se", p.cur("s_se"))
                se_mid = p.cur("s_se")
                OP("sync", lambda e, d=v_oloc[:, :, tc * ATC:(tc + 1) * ATC],
                   s=oarn[:, 0:MYH * ATC].rearrange("p (f t) -> p f t", f=MYH):
                   e.dma_start(out=d, in_=s),
                   waits=[("s_se", se_mid)], inc="s_out", inc_by=16)
                state["o_stored"] = ("s_out", p.cur("s_out"))
            state["o_done"] = ("s_out", p.cur("s_out"))

        def emit_attn_out(x1_view, dst_view, o_src, entry_waits):
            obp = [big[:, 0:8192], big[:, 8192:16384]]
            zbp = [xln[:, 0:8192], xln[:, 8192:16384]]
            woslab = [warn[:, 0:8192], warn[:, 8192:16384]]
            cst_done = ("s_cst", p.cur("s_cst"))
            sf = {}
            AH = ATT // 2

            def emit_fetch(oc):
                par = oc % 2
                obufv = obp[par].rearrange("p (h s) -> p h s", h=2)
                ow = list(sf.get(("ob", par), ())) + entry_waits
                OP("sync", lambda e, d=obufv[:, 0:1, :],
                   s=o_src[:, 2 * oc:2 * oc + 1, :]: e.dma_start(out=d, in_=s),
                   waits=ow, inc="s_x", inc_by=16)
                old = p.cur("s_x")
                OP("gpsimd", lambda e, d=obufv[:, 1:2, :],
                   s=o_src[:, 2 * oc + 1:2 * oc + 2, :]: e.dma_start(out=d, in_=s),
                   waits=ow, inc="s_gp", inc_by=16)
                og = ("s_gp", p.cur("s_gp"))
                zvw = zbp[par].rearrange("p (j s) -> p j s", j=16)
                zw = [sf.get(("zb", par), ("s_pe", 0))]
                for h2 in range(2):
                    for j in range(16):
                        eng = "vector" if j % 2 == 0 else "scalar"
                        w = []
                        if j < 2 and h2 == 0:
                            w = [("s_x", old) if h2 == 0 else og] + zw
                        if j < 2 and h2 == 1:
                            w = [og]
                        if h2 == 1 and j < 2:
                            w = [og] + zw
                        if eng == "vector":
                            OP("vector", lambda e, h2=h2, j=j, o=obufv, z=zvw: e.tensor_copy(
                                z[:, j, h2 * 256:(h2 + 1) * 256],
                                o[:, h2, j:4096:16]),
                               waits=w, inc="s_ve" if (h2 == 1 and j == 14) else None)
                        else:
                            OP("scalar", lambda e, h2=h2, j=j, o=obufv, z=zvw: e.activation(
                                z[:, j, h2 * 256:(h2 + 1) * 256],
                                o[:, h2, j:4096:16], AF.Identity),
                               waits=w, inc="s_se" if (h2 == 1 and j == 15) else None)
                sf[("zrdy", par)] = (("s_ve", p.cur("s_ve")), ("s_se", p.cur("s_se")))
                return None

            emit_fetch(0)
            for otc in range(4):
                par = otc % 2
                def x1_fn(e, otc=otc):
                    for arm in e.switch_core_id(n=128):
                        mhalf = int(arm.logical) % 2
                        e.dma_start(
                            out=xch[:, 0:AH * D].rearrange("p (t d) -> p t d", t=AH),
                            in_=x1_view[:, mhalf * (NTOK // 128) + otc * ATT:
                                        mhalf * (NTOK // 128) + otc * ATT + AH, :],
                        ).then_inc(sems["s_x"], 16)
                    return None
                def x1g_fn(e, otc=otc):
                    for arm in e.switch_core_id(n=128):
                        mhalf = int(arm.logical) % 2
                        e.dma_start(
                            out=xch[:, AH * D:ATT * D].rearrange("p (t d) -> p t d", t=AH),
                            in_=x1_view[:, mhalf * (NTOK // 128) + otc * ATT + AH:
                                        mhalf * (NTOK // 128) + (otc + 1) * ATT, :],
                        ).then_inc(sems["s_gp"], 16)
                    return None
                lw = [("s_out", p.cur("s_out")), state.get("st_gp", ("s_gp", 0))] + \
                     (entry_waits if otc == 0 else [])
                OP("sync", x1_fn, waits=lw, inc="s_x", inc_by=16)
                xld = p.cur("s_x")
                OP("gpsimd", x1g_fn, waits=lw, inc="s_gp", inc_by=16)
                xg1 = p.cur("s_gp")
                if otc < 3:
                    emit_fetch(otc + 1)
                ve_z, se_z = sf[("zrdy", par)]
                zbuf = zbp[par]
                for dc in range(4):
                    wpar = dc % 2
                    qeng, psem = (("gpsimd", "s_gp") if wpar == 0 else
                                  ("sync", "s_wb"))
                    OP(qeng, lambda e, s=d_pjc[10 + dc], d=woslab[wpar]:
                       e.dma_start(out=d, in_=s),
                       waits=[sf.get(("wo", wpar), ("s_pe", 0))] +
                             ([cst_done] + entry_waits if (otc == 0 and dc < 2) else []),
                       inc=psem, inc_by=16)
                    ldw = p.cur(psem)
                    for tt in range(ATT):
                        bi = 4 + (dc * ATT + tt) % 2
                        w = [(psem, ldw), ve_z, se_z, psum_free[bi]]
                        for jj in range(H):
                            OP("tensor", lambda e,
                               l=zbuf[:, jj * ATC + tt * 128: jj * ATC + tt * 128 + 128],
                               r=woslab[wpar][:, jj * 512:(jj + 1) * 512],
                               o=PS[bi][:, :], s=(jj == 0):
                               e.matmul(o, l, r, start=s, stop=False),
                               waits=w if jj == 0 else [])
                        OP("tensor", lambda e, o=PS[bi][:, :],
                           r=bo_row[0:1, dc * 512:(dc + 1) * 512]:
                           e.matmul(o, ones1b[0:1, :], r, start=False, stop=True),
                           inc="s_pe")
                        pe_w = p.cur("s_pe")
                        if tt == ATT - 1:
                            sf[("wo", wpar)] = ("s_pe", pe_w)
                        OP("vector", lambda e,
                           xp=xch[:, tt * D + dc * 512: tt * D + (dc + 1) * 512],
                           ps_=PS[bi][:, :]: e.tensor_tensor(xp, xp, ps_, ALU.add),
                           waits=[("s_pe", pe_w),
                                  ("s_x", xld) if tt < AH else ("s_gp", xg1)],
                           inc="s_ve")
                        psum_free[bi] = ("s_ve", p.cur("s_ve"))
                pe_done = ("s_pe", p.cur("s_pe"))
                sf[("ob", par)] = (pe_done,)
                sf[("zb", par)] = pe_done
                ve_all = p.cur("s_ve")
                OP("sync", lambda e, d=dst_view[:, otc * ATT:otc * ATT + AH, :]: e.dma_start(
                    out=d, in_=xch[:, 0:AH * D].rearrange("p (t d) -> p t d", t=AH)),
                   waits=[("s_ve", ve_all)], inc="s_out", inc_by=16)
                OP("gpsimd", lambda e, d=dst_view[:, otc * ATT + AH:(otc + 1) * ATT, :]:
                   e.dma_start(out=d, in_=xch[:, AH * D:ATT * D].rearrange(
                       "p (t d) -> p t d", t=AH)),
                   waits=[("s_ve", ve_all)], inc="s_gp", inc_by=16)
                state["st_gp"] = ("s_gp", p.cur("s_gp"))

        entry = [("s_pe", p.cur("s_pe")), ("s_ve", p.cur("s_ve")), ("s_se", p.cur("s_se"))]
        emit_ffn(v_x, v_xm1f, do_quant=True, entry_waits=entry, ntc=SEQ // FTC)
        entry = [("s_pe", p.cur("s_pe")), ("s_ve", p.cur("s_ve")), ("s_se", p.cur("s_se"))]
        emit_attn_src(v_xm1f, entry)
        entry = [("s_pe", p.cur("s_pe")), ("s_ve", p.cur("s_ve")), ("s_se", p.cur("s_se")),
                 state["o_done"]]
        emit_attn_out(v_xm1f, v_xm2, v_oloc, entry)
        entry = [("s_pe", p.cur("s_pe")), ("s_ve", p.cur("s_ve")), ("s_se", p.cur("s_se"))]
        emit_ffn(v_xm2, v_y, do_quant=False, entry_waits=entry, ntc=2)

        OP("sync", lambda e: e.nop(), waits=[("s_out", p.cur("s_out")),
                                             ("s_cst", p.cur("s_cst")),
                                             ("s_gp", p.cur("s_gp")),
                                             ("s_gp2", p.cur("s_gp2")),
                                             ("s_wa", p.cur("s_wa")),
                                             ("s_wb", p.cur("s_wb"))])

        @block.sync
        def _(eng):
            p.emit(eng, "sync", sems)

        @block.gpsimd
        def _(eng):
            p.emit(eng, "gpsimd", sems)

        @block.tensor
        def _(eng):
            p.emit(eng, "tensor", sems)

        @block.vector
        def _(eng):
            p.emit(eng, "vector", sems)

        @block.scalar
        def _(eng):
            p.emit(eng, "scalar", sems)

    return nc


def _host_prep(inputs):
    x = np.ascontiguousarray(np.asarray(inputs["L0_attn_x"], dtype=np.float32).reshape(4, SEQ, D))
    wqT = np.ascontiguousarray(np.asarray(inputs["L0_attn_Wq"], dtype=np.float32).T)
    wkT = np.ascontiguousarray(np.asarray(inputs["L0_attn_Wk"], dtype=np.float32).T)
    wvT = np.ascontiguousarray(np.asarray(inputs["L0_attn_Wv"], dtype=np.float32).T)
    woT = np.ascontiguousarray(np.asarray(inputs["L0_attn_Wo"], dtype=np.float32).T)
    w1T = np.zeros((D, HID), dtype=np.float32)
    w1T[:, :HID_REAL] = np.asarray(inputs["L0_ffn_w1"], dtype=np.float32).T
    w3T = np.zeros((D, HID), dtype=np.float32)
    w3T[:, :HID_REAL] = np.asarray(inputs["L0_ffn_w3"], dtype=np.float32).T
    w2T = np.zeros((HID, D), dtype=np.float32)
    w2T[:HID_REAL, :] = np.asarray(inputs["L0_ffn_w2"], dtype=np.float32).T
    bq = np.asarray(inputs["L0_attn_bq"], dtype=np.float32)
    bk = np.asarray(inputs["L0_attn_bk"], dtype=np.float32)
    bv = np.asarray(inputs["L0_attn_bv"], dtype=np.float32)
    bo = np.asarray(inputs["L0_attn_bo"], dtype=np.float32)
    bo_row = bo[None, :].astype(ml_dtypes.bfloat16)
    ones1b = np.ones((1, 128), dtype=np.float32).astype(ml_dtypes.bfloat16)
    ident = np.eye(128, dtype=np.float32).astype(ml_dtypes.bfloat16)
    m = np.arange(128)[:, None]
    n = np.arange(256)[None, :]
    maskB = np.where((m % GT) == (n % GT), 0.0, -1e9).astype(np.float32)
    cst = np.tile(np.array([[EPS, -0.5, 1.0 / np.sqrt(128.0), 0.0]], dtype=np.float32), (128, 1))
    ones1 = np.ones((1, 128), dtype=np.float32)
    ones128 = np.ones((128, 1), dtype=np.float32)
    shared = dict(wkT=wkT, wvT=wvT, woT=woT, w1T=w1T, w3T=w3T, w2T=w2T,
                  bo_row=bo_row, ones1b=ones1b, ident=ident, maskB=maskB,
                  cst=cst, ones1=ones1, ones128=ones128)
    in_maps = []
    for c in range(NCORES):
        sq, hf = c // 2, c % 2
        m_ = dict(shared)
        m_["x_in"] = np.ascontiguousarray(x[sq])
        m_["wqTm"] = np.ascontiguousarray(wqT[:, hf * 8 * 128:(hf * 8 + 8) * 128])
        bqkv = np.zeros((128, 40), dtype=np.float32)
        bqkv[:, 0:8] = bq.reshape(16, 128).T[:, hf * 8:(hf + 1) * 8]
        bqkv[:, 8:24] = bk.reshape(16, 128).T
        bqkv[:, 24:40] = bv.reshape(16, 128).T
        m_["bqkv"] = bqkv
        in_maps.append(m_)
    return in_maps


def kernel(**inputs):
    in_maps = _host_prep(inputs)
    nc = build_nc("fused")
    res = run_bass_kernel_spmd(nc, in_maps, core_ids=list(range(NCORES)))
    y = np.stack([np.asarray(r["y_out"]) for r in res.results], axis=0)
    return np.ascontiguousarray(y.reshape(4, SEQ, D)).astype(np.float32)


# version 09
# speedup vs baseline: 1.0299x; 1.0299x over previous
import contextlib
import numpy as np
import ml_dtypes
import concourse.bass as bass
import concourse.mybir as mybir
from concourse.bass_utils import run_bass_kernel_spmd

F32 = mybir.dt.float32
BF16 = mybir.dt.bfloat16
AF = mybir.ActivationFunctionType
ALU = mybir.AluOpType
AX = mybir.AxisListType

NCORES = 8
D = 2048
DK = D // 128
HID = 5504
HC = HID // 128
HID_REAL = 5461
NTOK = 2048
SEQ = 4096
FTC = 1024
FTT = FTC // 128
ATC = 512
ATT = ATC // 128
GT = 16
NG = ATC // GT
H = 16
MYH = 8
EPS = 1e-5
CNT_REAL = float(HID_REAL * D)
SEGS = [8, 8, 8, 8, 8, 3]
assert sum(SEGS) == HC

ENGS = ("sync", "gpsimd", "tensor", "vector", "scalar")
_PHASES = ("ffn1", "attn", "ffn2")


class Plan:

    def __init__(self):
        self.ops = {e: [] for e in ENGS}
        self.counts = {}
        self.waited = {e: {} for e in ENGS}

    def sem(self, name):
        self.counts.setdefault(name, 0)

    def op(self, eng, fn, waits=(), inc=None, inc_by=1):
        waits = list(waits)
        if inc is not None and inc_by == 16:
            waits.append((inc, self.counts[inc]))
        pruned = []
        w = self.waited[eng]
        for s, v in waits:
            if v > w.get(s, 0):
                w[s] = v
                pruned.append((s, v))
        if inc is not None:
            self.counts[inc] += inc_by
        self.ops[eng].append((fn, tuple(pruned), inc, inc_by))
        return self.counts[inc] if inc is not None else None

    def cur(self, name):
        return self.counts[name]

    def emit(self, eng_obj, eng_name, sems):
        for fn, waits, inc, inc_by in self.ops[eng_name]:
            for s, v in waits:
                eng_obj.wait_ge(sems[s], v)
            r = fn(eng_obj)
            if inc is not None and r is not None:
                r.then_inc(sems[inc], inc_by)


def build_nc(mode="fused"):
    nc = bass.Bass("TRN2", target_bir_lowering=False, debug=False)

    d_x = nc.dram_tensor("x_in", [SEQ, D], F32, kind="ExternalInput").ap()
    d_y = nc.dram_tensor("y_out", [NTOK, D], F32, kind="ExternalOutput").ap()
    d_wq = nc.dram_tensor("wqTm", [D, MYH * 128], F32, kind="ExternalInput").ap()
    d_wk = nc.dram_tensor("wkT", [D, D], F32, kind="ExternalInput").ap()
    d_wv = nc.dram_tensor("wvT", [D, D], F32, kind="ExternalInput").ap()
    d_wo = nc.dram_tensor("woT", [D, D], F32, kind="ExternalInput").ap()
    d_w1 = nc.dram_tensor("w1T", [D, HID], F32, kind="ExternalInput").ap()
    d_w3 = nc.dram_tensor("w3T", [D, HID], F32, kind="ExternalInput").ap()
    d_w2 = nc.dram_tensor("w2T", [HID, D], F32, kind="ExternalInput").ap()
    d_bqkv = nc.dram_tensor("bqkv", [128, 40], F32, kind="ExternalInput").ap()
    d_bo = nc.dram_tensor("bo_row", [1, D], BF16, kind="ExternalInput").ap()
    d_ones1b = nc.dram_tensor("ones1b", [1, 128], BF16, kind="ExternalInput").ap()
    d_ident = nc.dram_tensor("ident", [128, 128], BF16, kind="ExternalInput").ap()
    d_mask = nc.dram_tensor("maskB", [128, 256], F32, kind="ExternalInput").ap()
    d_cst = nc.dram_tensor("cst", [128, 4], F32, kind="ExternalInput").ap()
    d_ones1 = nc.dram_tensor("ones1", [1, 128], F32, kind="ExternalInput").ap()
    d_ones128 = nc.dram_tensor("ones128", [128, 1], F32, kind="ExternalInput").ap()
    d_xm1 = nc.dram_tensor("xm1", [SEQ, D], F32, kind="Internal").ap()
    d_xm2 = nc.dram_tensor("xm2", [NTOK, D], F32, kind="Internal").ap()
    d_oloc = nc.dram_tensor("oloc", [MYH, 128, SEQ], BF16, kind="Internal").ap()

    d_q1c = nc.dram_tensor("q1c", [HC, 128, D], BF16, kind="Internal").ap()
    d_pjc = nc.dram_tensor("pjc", [14, 128, DK * 512], BF16, kind="Internal").ap()
    d_q3c = nc.dram_tensor("q3c", [HC, 128, D], BF16, kind="Internal").ap()
    d_q2c = nc.dram_tensor("q2c", [HC, 128, D], BF16, kind="Internal").ap()

    v_w1 = d_w1.rearrange("(c p) f -> p c f", p=128)
    v_w3 = d_w3.rearrange("(c p) f -> p c f", p=128)
    v_wqm = d_wq.rearrange("(c p) f -> p c f", p=128)
    v_wk = d_wk.rearrange("(c p) f -> p c f", p=128)
    v_wv = d_wv.rearrange("(c p) f -> p c f", p=128)
    v_wo = d_wo.rearrange("(c p) f -> p c f", p=128)
    v_x = d_x.rearrange("(t p) d -> p t d", p=128)
    v_xm1f = d_xm1.rearrange("(t p) d -> p t d", p=128)
    v_oloc = d_oloc.rearrange("h d s -> d h s")
    v_y = d_y.rearrange("(t p) d -> p t d", p=128)
    v_xm2 = d_xm2.rearrange("(t p) d -> p t d", p=128)
    fws = [d_w1.flatten().rearrange("(n p f) -> n p f", p=128, f=2048),
           d_w3.flatten().rearrange("(n p f) -> n p f", p=128, f=2048),
           d_w2.flatten().rearrange("(n p f) -> n p f", p=128, f=2048)]

    p = Plan()
    for s in ("s_x", "s_wa", "s_wb", "s_gp", "s_gp2", "s_pe", "s_ve", "s_se",
              "s_out", "s_fl", "s_cst"):
        p.sem(s)
    OP = p.op

    es = contextlib.ExitStack()
    with es:
        sbt = lambda name, shape, dt: es.enter_context(nc.sbuf_tensor(name, shape, dt))
        xch = sbt("xch", [128, FTT * D], F32)
        xln = sbt("xln", [128, DK * FTC], BF16)
        big = sbt("big", [128, 24576], BF16)
        oarn = sbt("oarn", [128, 8192], BF16)
        warn = sbt("warn", [128, 16384], BF16)
        abuf = sbt("abuf", [128, 4096], BF16)
        bo_row = sbt("bo_rows", [1, D], BF16)
        ones1b = sbt("ones1bs", [1, 128], BF16)
        stats = sbt("stats", [128, 64], F32)
        sbc = sbt("sbc", [128, 8], F32)
        prt = sbt("prt", [128, 144], F32)
        pr1 = sbt("pr1", [128, 4], F32)
        sv3 = sbt("sv3", [1, 16], F32)
        bqkv = sbt("bqkvs", [128, 40], F32)
        cst = sbt("csts", [128, 4], F32)
        ident = sbt("idents", [128, 128], BF16)
        maskB = sbt("maskBs", [128, 256], F32)
        ones1 = sbt("ones1s", [1, 128], F32)
        ones128 = sbt("ones128s", [128, 1], F32)
        nbuf = sbt("nbuf", [1, 32], mybir.dt.int32)
        fbuf = sbt("fbuf", [1, 32], mybir.dt.int32)
        asc8 = abuf[:, 0:2048].bitcast(F32)
        a8 = abuf[:, 2048:3072]
        at_sb = abuf[:, 3072:3328]
        vt_sb = abuf[:, 3328:3584]

        PS = [es.enter_context(nc.psum_tensor(f"ps{i}", [128, 512], F32)) for i in range(8)]
        sems = {s: es.enter_context(nc.semaphore(s)) for s in
                ("s_x", "s_wa", "s_wb", "s_gp", "s_gp2", "s_pe", "s_ve", "s_se",
                 "s_out", "s_fl", "s_cst")}
        block = es.enter_context(nc.Block())

        psum_free = {i: ("s_ve", 0) for i in range(8)}
        state = {"lntmp": ("s_ve", 0)}
        lntmp_f = warn[:, 12288:14336]
        scr_f = warn[:, 14336:16384].bitcast(F32)
        lntmp_a = xln[:, 12288:14336]
        scr_a = xln[:, 14336:16384].bitcast(F32)

        for dst, src in ((bqkv, d_bqkv), (cst, d_cst), (ident, d_ident),
                         (maskB, d_mask), (ones1, d_ones1), (ones128, d_ones128),
                         (bo_row, d_bo), (ones1b, d_ones1b)):
            OP("sync", lambda e, d=dst, s=src: e.dma_start(out=d[:, :], in_=s[:, :]),
               inc="s_x", inc_by=16)
        preload_done = p.cur("s_x")

        slabs = [oarn[:, 0:4096].bitcast(F32), oarn[:, 4096:8192].bitcast(F32)]
        consumed = {}
        ci = 0
        for wi in range(3):
            for j in range(43):
                par = ci % 2
                psem = "s_wa" if par == 0 else "s_wb"
                waits = [consumed[par]] if par in consumed else []
                OP("sync", lambda e, s=slabs[par], src=fws[wi][j]: e.dma_start(out=s, in_=src),
                   waits=waits, inc=psem, inc_by=16)
                ld = p.cur(psem)
                OP("vector", lambda e, d=prt[:, wi * 48 + j: wi * 48 + j + 1], s=slabs[par]:
                   e.tensor_reduce(d, s, AX.X, ALU.add, apply_absolute_value=True),
                   waits=[(psem, ld)], inc="s_ve")
                consumed[par] = ("s_ve", p.cur("s_ve"))
                ci += 1
        OP("vector", lambda e: e.drain())
        for wi in range(3):
            OP("vector", lambda e, w=wi: e.tensor_reduce(
                pr1[:, w:w + 1], prt[:, w * 48:w * 48 + 43], AX.X, ALU.add),
               inc="s_ve" if wi == 2 else None)
        ve_pr = p.cur("s_ve")
        OP("tensor", lambda e: e.matmul(PS[0][0:1, 0:3], ones128[:, 0:1], pr1[:, 0:3],
                                        start=True, stop=True),
           waits=[("s_ve", ve_pr), ("s_x", preload_done)], inc="s_pe")
        pe_s = p.cur("s_pe")
        OP("vector", lambda e: e.tensor_scalar(sv3[0:1, 0:3], PS[0][0:1, 0:3],
                                               1.0 / CNT_REAL, None, ALU.mult),
           waits=[("s_pe", pe_s)], inc="s_ve")
        psum_free[0] = ("s_ve", p.cur("s_ve"))
        der = sv3[0:1, 8:16]
        OP("vector", lambda e: e.drain())
        OP("vector", lambda e: e.tensor_scalar(der[:, 0:1], sv3[0:1, 0:1], 0.5, None, ALU.mult))
        OP("vector", lambda e: e.tensor_tensor(der[:, 1:2], sv3[0:1, 2:3], sv3[0:1, 1:2], ALU.mult))
        specs = [(0, 0.5), (0, -0.5), (1, 0.5), (1, -0.5), (2, 0.5), (2, -0.5)]
        for k, (src_i, sc) in enumerate(specs):
            OP("vector", lambda e, k=k, si=src_i, sc=sc: e.tensor_scalar(
                der[:, 2 + k:3 + k], sv3[0:1, si:si + 1], sc, None, ALU.mult))
        OP("vector", lambda e: e.drain())
        OP("vector", lambda e: e.tensor_scalar(der[:, 1:2], der[:, 1:2], 0.125, None, ALU.mult))
        OP("vector", lambda e: e.drain())
        OP("vector", lambda e: e.nop(), inc="s_ve")
        ve_der = p.cur("s_ve")
        OP("tensor", lambda e: e.matmul(PS[1][:, 0:8], ones1[0:1, :], der,
                                        start=True, stop=True),
           waits=[("s_ve", ve_der)], inc="s_pe")
        pe_b = p.cur("s_pe")
        OP("vector", lambda e: e.tensor_copy(sbc[:], PS[1][:, 0:8]),
           waits=[("s_pe", pe_b)], inc="s_ve")
        ve_sbc = p.cur("s_ve")
        psum_free[1] = ("s_ve", ve_sbc)

        S1V = sbc[:, 0:1]
        GAM = sbc[:, 1:2]
        THR = {("w1", 0): sbc[:, 2:3], ("w1", 1): sbc[:, 3:4],
               ("w3", 0): sbc[:, 4:5], ("w3", 1): sbc[:, 5:6],
               ("w2", 0): sbc[:, 6:7], ("w2", 1): sbc[:, 7:8]}

        def emit_ln(ntt, tcw, extra_waits, lntmp, scr, half_waits=None):
            nm = stats[:, 0:ntt]
            sqa = stats[:, 16:16 + ntt]
            sqb = stats[:, 24:24 + ntt]
            rstd = stats[:, 32:32 + ntt]
            nmr = stats[:, 40:40 + ntt]
            scr2 = lntmp[:, 0:2048].bitcast(F32)
            lnbufs = [lntmp, scr.bitcast(BF16)]
            xlnv = xln[:, 0:DK * tcw].rearrange("p (c t) -> p c t", c=DK)
            for tt in range(ntt):
                xt = xch[:, tt * D:(tt + 1) * D]
                rw = []
                if tt == 0:
                    rw = extra_waits
                elif half_waits is not None and tt == ntt // 2:
                    rw = half_waits
                OP("vector", lambda e, d=nm[:, tt:tt + 1], s=xt: e.tensor_reduce(
                    d, s, AX.X, ALU.add, negate=True),
                   waits=rw, inc="s_ve")
                if tt == 0:
                    sq_w = extra_waits + [state.get("scr", ("s_ve", 0)),
                                          state["lntmp"], state.get("lntmp2", ("s_ve", 0))]
                else:
                    sq_w = [("s_se", p.cur("s_se"))]
                    if half_waits is not None and tt == ntt // 2:
                        sq_w = sq_w + half_waits
                OP("scalar", lambda e, d=sqa[:, tt:tt + 1], s=xt: e.activation(
                    scr[:, 0:1024], s[:, 0:1024], AF.Square, accum_out=d), waits=sq_w)
                OP("scalar", lambda e, d=sqb[:, tt:tt + 1], s=xt: e.activation(
                    scr2[:, 0:1024], s[:, 1024:2048], AF.Square, accum_out=d), inc="s_se")
            OP("scalar", lambda e: e.drain())
            OP("scalar", lambda e: e.nop(), inc="s_se")
            OP("vector", lambda e: e.drain())
            ve0, se0 = p.cur("s_ve"), p.cur("s_se")
            OP("vector", lambda e: e.tensor_scalar(nm, nm, 1.0 / D, None, ALU.mult),
               waits=[("s_se", se0)])
            OP("vector", lambda e: e.tensor_tensor(sqa, sqa, sqb, ALU.add))
            OP("vector", lambda e: e.drain())
            OP("vector", lambda e: e.tensor_tensor(sqb, nm, nm, ALU.mult))
            OP("vector", lambda e: e.tensor_scalar(sqa, sqa, 1.0 / D, None, ALU.mult))
            OP("vector", lambda e: e.drain())
            OP("vector", lambda e: e.tensor_tensor(sqa, sqa, sqb, ALU.subtract))
            OP("vector", lambda e: e.drain())
            OP("vector", lambda e: e.nop(), inc="s_ve")
            ve1 = p.cur("s_ve")
            OP("scalar", lambda e: e.activation(rstd, sqa, AF.Ln, bias=cst[:, 0:1]),
               waits=[("s_ve", ve1)], inc="s_se")
            OP("scalar", lambda e: e.activation(rstd, rstd, AF.Exp, scale=cst[:, 1:2]),
               waits=[("s_se", p.cur("s_se"))], inc="s_se")
            se1 = p.cur("s_se")
            OP("vector", lambda e: e.tensor_tensor(nmr, nm, rstd, ALU.mult),
               waits=[("s_se", se1)], inc="s_ve")
            ve2 = p.cur("s_ve")
            bmark = [state["lntmp"], state.get("lntmp2", ("s_ve", 0))]
            for tt in range(ntt):
                xt = xch[:, tt * D:(tt + 1) * D]
                buf = lnbufs[tt % 2]
                OP("scalar", lambda e, s=xt, tt=tt, b=buf: e.activation(
                    b[:, 0:D], s, AF.Identity, bias=nmr[:, tt:tt + 1], scale=rstd[:, tt:tt + 1]),
                   waits=[("s_ve", ve2), bmark[tt % 2]], inc="s_se")
                se_t = p.cur("s_se")
                for dk4 in range(4):
                    bi = 6 + (tt * 4 + dk4) % 2
                    bview = PS[bi][:, :].bitcast(BF16)
                    for j in range(4):
                        dk = dk4 * 4 + j
                        OP("tensor", lambda e, bv=bview, j=j, dk=dk, b=buf: e.transpose(
                            bv[:, j * 128:(j + 1) * 128],
                            b[:, dk * 128:(dk + 1) * 128], ident[:]),
                           waits=[("s_se", se_t), psum_free[bi]] if j == 0 else [],
                           inc="s_pe" if j == 3 else None)
                    pe_t = p.cur("s_pe")
                    dst = xlnv[:, dk4 * 4:dk4 * 4 + 4, tt * 128:(tt + 1) * 128]
                    OP("vector", lambda e, d=dst, bv=bview: e.tensor_copy(
                        d, bv[:, 0:512].rearrange("p (c t) -> p c t", c=4)),
                       waits=[("s_pe", pe_t)], inc="s_ve")
                    psum_free[bi] = ("s_ve", p.cur("s_ve"))
                bmark[tt % 2] = ("s_ve", p.cur("s_ve"))
            state["lntmp"] = bmark[0]
            state["lntmp2"] = bmark[1]
            state["scr"] = ("s_pe", p.cur("s_pe"))
            return ("s_ve", p.cur("s_ve"))

        cast_jobs = ([(f4, v_wqm, f4 * 512) for f4 in range(2)] +
                     [(2 + f4, v_wk, f4 * 512) for f4 in range(4)] +
                     [(6 + f4, v_wv, f4 * 512) for f4 in range(4)] +
                     [(10 + dc, v_wo, dc * 512) for dc in range(4)])

        def emit_ffn(src_view, dst_view, do_quant, entry_waits, ntc=2):
            is_ffn1 = do_quant
            q1slab = [warn[:, 0:2048], warn[:, 2048:4096]]
            q3slab = [warn[:, 4096:6144], warn[:, 6144:8192]]
            gseg = big[:, 0:8192]
            q2f = big[:, 8192:24576]
            f32slab = [oarn[:, 0:4096].bitcast(F32), oarn[:, 4096:8192].bitcast(F32)]
            acmp = [abuf[:, 0:2048], abuf[:, 2048:4096],
                    warn[:, 8192:10240], warn[:, 10240:12288]]
            sf = dict(state.get("ffn_sf", {}))

            FH = FTT // 2
            for tc in range(ntc):
                lw_s = [("s_out", p.cur("s_out"))] + \
                       ([state.get("st_gp", ("s_gp", 0))] + entry_waits if tc == 0 else [])
                lw_g = ([("s_out", p.cur("s_out")), state.get("st_gp", ("s_gp", 0))] +
                        entry_waits) if tc == 0 else []
                OP("sync", lambda e, s=src_view[:, tc * FTT:tc * FTT + FH, :]: e.dma_start(
                    out=xch[:, 0:FH * D].rearrange("p (t d) -> p t d", t=FH), in_=s),
                   waits=lw_s, inc="s_x", inc_by=16)
                xld = p.cur("s_x")
                OP("gpsimd", lambda e, s=src_view[:, tc * FTT + FH:(tc + 1) * FTT, :]:
                   e.dma_start(out=xch[:, FH * D:FTT * D].rearrange(
                       "p (t d) -> p t d", t=FH), in_=s),
                   waits=lw_g, inc="s_gp", inc_by=16)
                xg = p.cur("s_gp")
                xln_ready = emit_ln(FTT, FTC, [("s_x", xld)], lntmp_f, scr_f,
                                    half_waits=[("s_gp", xg)])

                hc0 = 0
                for seg, nq in enumerate(SEGS):
                    if is_ffn1 and not do_quant and cast_jobs:
                        cidx, cwv, coff = cast_jobs.pop(0)
                        OP("gpsimd", lambda e,
                           d=d_pjc[cidx].rearrange("p (c f) -> p c f", c=DK),
                           s=cwv[:, :, coff:coff + 512]: e.dma_start(out=d, in_=s),
                           inc="s_cst", inc_by=16)
                    def emit_q2_prefetch(nq=nq, hc0=hc0):
                        OP("sync", lambda e, d=q2f[:, 0:nq * 2048].rearrange(
                            "p (n f) -> p n f", n=nq),
                           c=d_q2c[hc0:hc0 + nq].rearrange("n p f -> p n f"):
                           e.dma_start(out=d, in_=c),
                           waits=[sf.get("q2h", ("s_pe", 0)),
                                  ("s_gp2", p.cur("s_gp2")) if do_quant else
                                  state.get("cache2", ("s_gp2", 0))],
                           inc="s_wa", inc_by=16)
                        return p.cur("s_wa")
                    if not do_quant:
                        q2ld = emit_q2_prefetch()
                    for hq in range(nq):
                        hc = hc0 + hq
                        par = hc % 2
                        pb = [(hc % 2) * 4 + i for i in range(4)]
                        if do_quant:
                            OP("sync", lambda e, src=v_w1[:, :, hc * 128:(hc + 1) * 128], d=f32slab[0]:
                               e.dma_start(out=d.rearrange("p (c f) -> p c f", c=DK), in_=src),
                               waits=[sf.get(("f32", 0), ("s_se", 0))], inc="s_wa", inc_by=16)
                            ld1 = p.cur("s_wa")
                            OP("sync", lambda e, src=v_w3[:, :, hc * 128:(hc + 1) * 128], d=f32slab[1]:
                               e.dma_start(out=d.rearrange("p (c f) -> p c f", c=DK), in_=src),
                               waits=[sf.get(("f32", 1), ("s_se", 0))], inc="s_wb", inc_by=16)
                            ld3 = p.cur("s_wb")
                            OP("scalar", lambda e: e.activation(
                                acmp[0], f32slab[0], AF.Sign, bias=THR[("w1", 1)]),
                               waits=[("s_wa", ld1), ("s_ve", ve_sbc),
                                      sf.get("ac01", ("s_ve", 0))])
                            OP("scalar", lambda e: e.activation(
                                acmp[1], f32slab[0], AF.Sign, bias=THR[("w1", 0)]),
                               inc="s_se")
                            se1 = p.cur("s_se")
                            OP("scalar", lambda e: e.activation(
                                acmp[2], f32slab[1], AF.Sign, bias=THR[("w3", 1)]),
                               waits=[("s_wb", ld3), sf.get("ac23", ("s_ve", 0))])
                            OP("scalar", lambda e: e.activation(
                                acmp[3], f32slab[1], AF.Sign, bias=THR[("w3", 0)]),
                               inc="s_se")
                            se13 = p.cur("s_se")
                            OP("vector", lambda e, q=q1slab[par]: e.tensor_tensor(
                                q, acmp[0], acmp[1], ALU.add),
                               waits=[("s_se", se13),
                                      sf.get(("w1q", par), ("s_pe", 0)),
                                      sf.get(("w3q", par), ("s_pe", 0)),
                                      ("s_gp", p.cur("s_gp"))])
                            OP("vector", lambda e, q=q3slab[par]: e.tensor_tensor(
                                q, acmp[2], acmp[3], ALU.add), inc="s_ve")
                            qve = p.cur("s_ve")
                            sf["ac23"] = ("s_ve", qve)
                            OP("gpsimd", lambda e, q=q1slab[par], c=d_q1c[hc]: e.dma_start(out=c, in_=q),
                               waits=[("s_ve", qve)], inc="s_gp", inc_by=16)
                            OP("gpsimd", lambda e, q=q3slab[par], c=d_q3c[hc]: e.dma_start(out=c, in_=q),
                               waits=[("s_ve", qve)], inc="s_gp", inc_by=16)
                            qready = [("s_ve", qve)]
                            OP("sync", lambda e, src=d_w2[hc * 128:(hc + 1) * 128, :], d=f32slab[0]:
                               e.dma_start(out=d, in_=src),
                               waits=[("s_se", se1)], inc="s_wb", inc_by=16)
                            ld2 = p.cur("s_wb")
                            OP("scalar", lambda e: e.activation(
                                acmp[0], f32slab[0], AF.Sign, bias=THR[("w2", 1)]),
                               waits=[("s_wb", ld2), ("s_ve", qve)])
                            OP("scalar", lambda e: e.activation(
                                acmp[1], f32slab[0], AF.Sign, bias=THR[("w2", 0)]),
                               inc="s_se")
                            se2 = p.cur("s_se")
                            sf[("f32", 0)] = ("s_se", se2)
                            sf[("f32", 1)] = ("s_se", se13)
                            OP("vector", lambda e, d=q2f[:, hq * 2048:(hq + 1) * 2048]:
                               e.tensor_tensor(d, acmp[0], acmp[1], ALU.add),
                               waits=[("s_se", se2), ("s_gp2", p.cur("s_gp2"))] +
                                     ([sf.get("q2h", ("s_pe", 0))] if hq == 0 else []),
                               inc="s_ve")
                            qve2 = p.cur("s_ve")
                            sf["ac01"] = ("s_ve", qve2)
                            OP("gpsimd", lambda e, q=q2f[:, hq * 2048:(hq + 1) * 2048],
                               c=d_q2c[hc]: e.dma_start(out=c, in_=q),
                               waits=[("s_ve", qve2)], inc="s_gp2", inc_by=16)
                            q2rdy = ("s_ve", qve2)
                        else:
                            if par == 0:
                                n2 = min(2, hc0 + nq - hc)
                                gate = entry_waits if (tc == 0 and hc == 0) else []
                                OP("gpsimd", lambda e, d=warn[:, 0:n2 * 2048].rearrange(
                                    "p (n f) -> p n f", n=n2),
                                   c=d_q1c[hc:hc + n2].rearrange("n p f -> p n f"):
                                   e.dma_start(out=d, in_=c),
                                   waits=gate + [sf.get(("w1q", 0), ("s_pe", 0)),
                                                 sf.get(("w1q", 1), ("s_pe", 0)),
                                                 state.get("cache", ("s_gp", 0)),
                                                 state.get("cache2", ("s_gp2", 0))],
                                   inc="s_gp", inc_by=16)
                                OP("gpsimd", lambda e, d=warn[:, 4096:4096 + n2 * 2048].rearrange(
                                    "p (n f) -> p n f", n=n2),
                                   c=d_q3c[hc:hc + n2].rearrange("n p f -> p n f"):
                                   e.dma_start(out=d, in_=c),
                                   waits=[sf.get(("w3q", 0), ("s_pe", 0)),
                                          sf.get(("w3q", 1), ("s_pe", 0))],
                                   inc="s_gp", inc_by=16)
                                state["qld"] = ("s_gp", p.cur("s_gp"))
                            qready = [state["qld"]]
                        mmw = qready + [xln_ready] + [psum_free[b] for b in pb]
                        for dk in range(DK):
                            for wsl, b0 in ((q1slab[par], 0), (q3slab[par], 2)):
                                for half in range(2):
                                    OP("tensor", lambda e, l=wsl[:, dk * 128:(dk + 1) * 128],
                                       r=xln[:, dk * FTC + half * 512: dk * FTC + (half + 1) * 512],
                                       o=PS[pb[b0 + half]][:, :], s=(dk == 0), st=(dk == DK - 1):
                                       e.matmul(o, l, r, start=s, stop=st),
                                       waits=mmw if (dk == 0 and b0 == 0 and half == 0) else [],
                                       inc="s_pe" if (dk == DK - 1 and b0 == 2 and half == 1) else None)
                        pe_h = p.cur("s_pe")
                        sf[("w1q", par)] = ("s_pe", pe_h)
                        sf[("w3q", par)] = ("s_pe", pe_h)
                        for half in range(2):
                            OP("scalar", lambda e, src=PS[pb[half]][:, :],
                               d=scr_f[:, half * 512:(half + 1) * 512]:
                               e.activation(d, src, AF.Silu, scale=S1V),
                               waits=[("s_pe", pe_h), sf.get(("scr", half), ("s_ve", 0)),
                                      state.get("scr", ("s_se", 0))], inc="s_se")
                            se_h = p.cur("s_se")
                            OP("vector", lambda e, d=gseg[:, hq * FTC + half * 512: hq * FTC + (half + 1) * 512],
                               sl=scr_f[:, half * 512:(half + 1) * 512], h3=PS[pb[2 + half]][:, :]:
                               e.scalar_tensor_tensor(d, sl, GAM, h3, ALU.mult, ALU.mult),
                               waits=[("s_se", se_h)], inc="s_ve")
                            sf[("scr", half)] = ("s_ve", p.cur("s_ve"))
                        ve_g = p.cur("s_ve")
                        for b in pb:
                            psum_free[b] = ("s_ve", ve_g)
                    state["scr"] = ("s_ve", 0)
                    for tt in range(FTT):
                        for dc in range(4):
                            b = (tt % 2) * 4 + dc
                            w = ([] if do_quant else [("s_wa", q2ld)]) + \
                                [("s_ve", ve_g), psum_free[b]]
                            for hq in range(nq):
                                OP("tensor", lambda e,
                                   l=gseg[:, hq * FTC + tt * 128: hq * FTC + tt * 128 + 128],
                                   r=q2f[:, hq * 2048 + dc * 512: hq * 2048 + (dc + 1) * 512],
                                   o=PS[b][:, :], s=(hq == 0), st=(hq == nq - 1):
                                   e.matmul(o, l, r, start=s, stop=st),
                                   waits=w if hq == 0 else [],
                                   inc="s_pe" if hq == nq - 1 else None)
                            pe_y = p.cur("s_pe")
                            OP("vector", lambda e,
                               xp=xch[:, tt * D + dc * 512: tt * D + (dc + 1) * 512],
                               ps_=PS[b][:, :]: e.tensor_tensor(xp, xp, ps_, ALU.add),
                               waits=[("s_pe", pe_y)], inc="s_ve")
                            psum_free[b] = ("s_ve", p.cur("s_ve"))
                    sf["q2h"] = ("s_pe", p.cur("s_pe"))
                    hc0 += nq
                ve_all = p.cur("s_ve")
                OP("sync", lambda e, d=dst_view[:, tc * FTT:tc * FTT + FH, :]: e.dma_start(
                    out=d, in_=xch[:, 0:FH * D].rearrange("p (t d) -> p t d", t=FH)),
                   waits=[("s_ve", ve_all)], inc="s_out", inc_by=16)
                OP("gpsimd", lambda e, d=dst_view[:, tc * FTT + FH:(tc + 1) * FTT, :]:
                   e.dma_start(out=d, in_=xch[:, FH * D:FTT * D].rearrange(
                       "p (t d) -> p t d", t=FH)),
                   waits=[("s_ve", ve_all)], inc="s_gp", inc_by=16)
                state["st_gp"] = ("s_gp", p.cur("s_gp"))
                if do_quant and tc == 0:
                    state["cache"] = ("s_gp", p.cur("s_gp"))
                    state["cache2"] = ("s_gp2", p.cur("s_gp2"))
                    state["cache2q"] = ("s_gp2", p.cur("s_gp2"))
                    do_quant = False
            state["ffn_sf"] = {}

        def emit_attn_src(src_view, entry_waits):
            qpk = big[:, 0:4096]
            kpk = big[:, 4096:12288]
            vpk = big[:, 12288:20480]
            vt_sb2 = big[:, 20480:21504]
            wslab4 = [warn[:, 0:8192], warn[:, 8192:16384]]
            sf = {}
            asc4 = abuf[:, 0:2048].bitcast(F32)
            a4 = abuf[:, 2048:3072]
            at_sb2 = abuf[:, 3072:3584]
            mx4 = stats[:, 48:52]
            sm4 = stats[:, 52:56]
            projs = [(0, MYH, qpk, 0, 0, 8), (1, H, kpk, 2, 8, 16), (2, H, vpk, 6, 24, 16)]
            cst_done = ("s_cst", p.cur("s_cst"))
            AH = ATT // 2
            for tc in range(SEQ // ATC):
                lw = [state.get("axf", ("s_ve", 0)), state.get("st_gp", ("s_gp", 0))] + \
                     (entry_waits + [("s_out", p.cur("s_out"))] if tc == 0 else [])
                OP("sync", lambda e, s=src_view[:, tc * ATT:tc * ATT + AH, :]: e.dma_start(
                    out=xch[:, 0:AH * D].rearrange("p (t d) -> p t d", t=AH), in_=s),
                   waits=lw, inc="s_x", inc_by=16)
                xld = p.cur("s_x")
                OP("gpsimd", lambda e, s=src_view[:, tc * ATT + AH:(tc + 1) * ATT, :]:
                   e.dma_start(out=xch[:, AH * D:ATT * D].rearrange(
                       "p (t d) -> p t d", t=AH), in_=s),
                   waits=lw, inc="s_gp", inc_by=16)
                xg = p.cur("s_gp")
                xln_ready = emit_ln(ATT, ATC, [("s_x", xld)], lntmp_a, scr_a,
                                    half_waits=[("s_gp", xg)])
                state["axf"] = xln_ready
                ti = 0
                for pi, ntile, pack, base, boff, tgrp in projs:
                    for f4 in range(ntile // 4):
                        par = ti % 2
                        qeng, psem = (("gpsimd", "s_gp") if par == 0 else
                                      ("sync", "s_wb"))
                        OP(qeng, lambda e, s=d_pjc[base + f4], d=wslab4[par]:
                           e.dma_start(out=d, in_=s),
                           waits=[sf.get(("pw", par), ("s_pe", 0))] +
                                 ([cst_done] + entry_waits if (tc == 0 and ti < 2) else []),
                           inc=psem, inc_by=16)
                        ldw = p.cur(psem)
                        wsv = wslab4[par].rearrange("p (c f) -> p c f", c=DK)
                        for fi in range(4):
                            f = f4 * 4 + fi
                            bi = (ti * 4 + fi) % 4
                            mw = [(psem, ldw), xln_ready, psum_free[bi]]
                            for dk in range(DK):
                                OP("tensor", lambda e, l=wsv[:, dk, fi * 128:(fi + 1) * 128],
                                   r=xln[:, dk * ATC:dk * ATC + ATC],
                                   o=PS[bi][:, :], s=(dk == 0), st=(dk == DK - 1):
                                   e.matmul(o, l, r, start=s, stop=st),
                                   waits=mw if dk == 0 else [], inc="s_pe" if dk == DK - 1 else None)
                            pe_p = p.cur("s_pe")
                            if fi == 3:
                                sf[("pw", par)] = ("s_pe", pe_p)
                            dstv = pack.rearrange("p (g f t) -> p g f t", g=NG, f=tgrp)[:, :, f, :]
                            srcv = PS[bi][:, :].rearrange("p (g t) -> p g t", g=NG)
                            OP("scalar", lambda e, d=dstv, s=srcv,
                               b=bqkv[:, boff + f: boff + f + 1]:
                               e.activation(d, s, AF.Identity, bias=b),
                               waits=[("s_pe", pe_p)], inc="s_se")
                            psum_free[bi] = ("s_se", p.cur("s_se"))
                        ti += 1
                se_packs = p.cur("s_se")
                for b in range(NG // 4):
                    OP("vector", lambda e: e.drain())
                    for i in range(4):
                        g = b * 4 + i
                        par = g % 2
                        OP("tensor", lambda e, l=qpk[:, g * 128:(g + 1) * 128],
                           r=kpk[:, g * 256:(g + 1) * 256], o=PS[par][:, 0:256]:
                           e.matmul(o, l, r, start=True, stop=True),
                           waits=[("s_se", se_packs), psum_free[par]], inc="s_pe")
                        pe_g = p.cur("s_pe")
                        w = [("s_pe", pe_g)]
                        if b > 0 and i == 0:
                            w.append(state["a8free"])
                        OP("vector", lambda e, i=i, s=PS[par][:, 0:256]: e.scalar_tensor_tensor(
                            asc4[:, i * 256:(i + 1) * 256], s, cst[:, 2:3], maskB[:],
                            ALU.mult, ALU.add), waits=w, inc="s_ve")
                        psum_free[par] = ("s_ve", p.cur("s_ve"))
                    OP("vector", lambda e: e.drain())
                    for i in range(4):
                        OP("vector", lambda e, i=i: e.tensor_reduce(
                            mx4[:, i:i + 1], asc4[:, i * 256:(i + 1) * 256], AX.X, ALU.max,
                            negate=True), inc="s_ve" if i == 3 else None)
                    ve_rm = p.cur("s_ve")
                    for i in range(4):
                        OP("scalar", lambda e, i=i: e.activation(
                            asc4[:, i * 256:(i + 1) * 256], asc4[:, i * 256:(i + 1) * 256],
                            AF.Exp, bias=mx4[:, i:i + 1]),
                           waits=[("s_ve", ve_rm)] if i == 0 else [],
                           inc="s_se" if i == 3 else None)
                    se_exp = p.cur("s_se")
                    for i in range(4):
                        OP("vector", lambda e, i=i: e.tensor_reduce(
                            sm4[:, i:i + 1], asc4[:, i * 256:(i + 1) * 256], AX.X, ALU.add),
                           waits=[("s_se", se_exp)] if i == 0 else [])
                    OP("vector", lambda e: e.drain())
                    OP("vector", lambda e: e.reciprocal(sm4, sm4))
                    OP("vector", lambda e: e.drain())
                    for i in range(4):
                        w = []
                        if i == 0 and "at_free" in state:
                            w = [state["at_free"]]
                        OP("vector", lambda e, i=i: e.tensor_scalar(
                            a4[:, i * 256:(i + 1) * 256], asc4[:, i * 256:(i + 1) * 256],
                            sm4[:, i:i + 1], None, ALU.mult),
                           waits=w, inc="s_ve" if i == 3 else None)
                    ve_norm = p.cur("s_ve")
                    for i in range(4):
                        g = b * 4 + i
                        par = g % 2
                        for half in range(2):
                            bb = (4 + par) if half == 0 else (6 + par)
                            OP("tensor", lambda e, o=PS[bb][:, :].bitcast(BF16)[:, 0:128],
                               i_=a4[:, i * 256 + half * 128: i * 256 + (half + 1) * 128]:
                               e.transpose(o, i_, ident[:]),
                               waits=([("s_ve", ve_norm)] if half == 0 else []) + [psum_free[bb]])
                            OP("tensor", lambda e, o=PS[bb][:, :].bitcast(BF16)[:, 128:256],
                               i_=vpk[:, g * 256 + half * 128: g * 256 + (half + 1) * 128]:
                               e.transpose(o, i_, ident[:]),
                               inc="s_pe" if half == 1 else None)
                        pe_t = p.cur("s_pe")
                        for half in range(2):
                            bb = (4 + par) if half == 0 else (6 + par)
                            OP("vector", lambda e, d=at_sb2[:, par * 256 + half * 128: par * 256 + (half + 1) * 128],
                               s=PS[bb][:, :].bitcast(BF16)[:, 0:128]: e.tensor_copy(d, s),
                               waits=[("s_pe", pe_t)] if half == 0 else [])
                            OP("vector", lambda e, d=vt_sb2[:, par * 256 + half * 128: par * 256 + (half + 1) * 128],
                               s=PS[bb][:, :].bitcast(BF16)[:, 128:256]: e.tensor_copy(d, s),
                               inc="s_ve" if half == 1 else None)
                        ve_t = p.cur("s_ve")
                        psum_free[4 + par] = ("s_ve", ve_t)
                        psum_free[6 + par] = ("s_ve", ve_t)
                        for half in range(2):
                            OP("tensor", lambda e, l=vt_sb2[:, par * 256 + half * 128: par * 256 + (half + 1) * 128],
                               r=at_sb2[:, par * 256 + half * 128: par * 256 + (half + 1) * 128],
                               o=PS[2 + par][:, 0:128], s=(half == 0), st=(half == 1):
                               e.matmul(o, l, r, start=s, stop=st),
                               waits=[("s_ve", ve_t), psum_free[2 + par]] if half == 0 else [],
                               inc="s_pe" if half == 1 else None)
                        pe_av = p.cur("s_pe")
                        if i == 3:
                            state["a8free"] = ("s_pe", pe_av)
                            state["at_free"] = ("s_pe", pe_av)
                        dsto = oarn[:, 0:MYH * ATC].rearrange("p (f t) -> p f t", f=MYH)[
                            :, :, g * GT:(g + 1) * GT]
                        OP("scalar", lambda e, d=dsto,
                           s=PS[2 + par][:, 0:128].rearrange("p (f t) -> p f t", f=MYH):
                           e.activation(d, s, AF.Identity),
                           waits=[("s_pe", pe_av), state.pop("o_stored", ("s_out", 0))],
                           inc="s_se")
                        psum_free[2 + par] = ("s_se", p.cur("s_se"))
                se_mid = p.cur("s_se")
                OP("sync", lambda e, d=v_oloc[:, :, tc * ATC:(tc + 1) * ATC],
                   s=oarn[:, 0:MYH * ATC].rearrange("p (f t) -> p f t", f=MYH):
                   e.dma_start(out=d, in_=s),
                   waits=[("s_se", se_mid)], inc="s_out", inc_by=16)
                state["o_stored"] = ("s_out", p.cur("s_out"))
            state["o_done"] = ("s_out", p.cur("s_out"))

        def emit_attn_out(x1_view, dst_view, o_src, entry_waits):
            obp = [big[:, 0:8192], big[:, 8192:16384]]
            zbp = [xln[:, 0:8192], xln[:, 8192:16384]]
            woslab = [warn[:, 0:8192], warn[:, 8192:16384]]
            cst_done = ("s_cst", p.cur("s_cst"))
            sf = {}
            AH = ATT // 2

            def emit_fetch(oc):
                par = oc % 2
                obufv = obp[par].rearrange("p (h s) -> p h s", h=2)
                ow = list(sf.get(("ob", par), ())) + entry_waits
                OP("sync", lambda e, d=obufv[:, 0:1, :],
                   s=o_src[:, 2 * oc:2 * oc + 1, :]: e.dma_start(out=d, in_=s),
                   waits=ow, inc="s_x", inc_by=16)
                old = p.cur("s_x")
                OP("gpsimd", lambda e, d=obufv[:, 1:2, :],
                   s=o_src[:, 2 * oc + 1:2 * oc + 2, :]: e.dma_start(out=d, in_=s),
                   waits=ow, inc="s_gp", inc_by=16)
                og = ("s_gp", p.cur("s_gp"))
                zvw = zbp[par].rearrange("p (j s) -> p j s", j=16)
                zw = [sf.get(("zb", par), ("s_pe", 0))]
                for h2 in range(2):
                    for j in range(16):
                        eng = "vector" if j % 2 == 0 else "scalar"
                        w = []
                        if j < 2 and h2 == 0:
                            w = [("s_x", old) if h2 == 0 else og] + zw
                        if j < 2 and h2 == 1:
                            w = [og]
                        if h2 == 1 and j < 2:
                            w = [og] + zw
                        if eng == "vector":
                            OP("vector", lambda e, h2=h2, j=j, o=obufv, z=zvw: e.tensor_copy(
                                z[:, j, h2 * 256:(h2 + 1) * 256],
                                o[:, h2, j:4096:16]),
                               waits=w, inc="s_ve" if (h2 == 1 and j == 14) else None)
                        else:
                            OP("scalar", lambda e, h2=h2, j=j, o=obufv, z=zvw: e.activation(
                                z[:, j, h2 * 256:(h2 + 1) * 256],
                                o[:, h2, j:4096:16], AF.Identity),
                               waits=w, inc="s_se" if (h2 == 1 and j == 15) else None)
                sf[("zrdy", par)] = (("s_ve", p.cur("s_ve")), ("s_se", p.cur("s_se")))
                return None

            emit_fetch(0)
            for otc in range(4):
                par = otc % 2
                def x1_fn(e, otc=otc):
                    for arm in e.switch_core_id(n=128):
                        mhalf = int(arm.logical) % 2
                        e.dma_start(
                            out=xch[:, 0:AH * D].rearrange("p (t d) -> p t d", t=AH),
                            in_=x1_view[:, mhalf * (NTOK // 128) + otc * ATT:
                                        mhalf * (NTOK // 128) + otc * ATT + AH, :],
                        ).then_inc(sems["s_x"], 16)
                    return None
                def x1g_fn(e, otc=otc):
                    for arm in e.switch_core_id(n=128):
                        mhalf = int(arm.logical) % 2
                        e.dma_start(
                            out=xch[:, AH * D:ATT * D].rearrange("p (t d) -> p t d", t=AH),
                            in_=x1_view[:, mhalf * (NTOK // 128) + otc * ATT + AH:
                                        mhalf * (NTOK // 128) + (otc + 1) * ATT, :],
                        ).then_inc(sems["s_gp"], 16)
                    return None
                lw_s = [("s_out", p.cur("s_out"))] + \
                       ([state.get("st_gp", ("s_gp", 0))] + entry_waits if otc == 0 else [])
                lw_g = ([("s_out", p.cur("s_out")), state.get("st_gp", ("s_gp", 0))] +
                        entry_waits) if otc == 0 else []
                OP("sync", x1_fn, waits=lw_s, inc="s_x", inc_by=16)
                xld = p.cur("s_x")
                OP("gpsimd", x1g_fn, waits=lw_g, inc="s_gp", inc_by=16)
                xg1 = p.cur("s_gp")
                if otc < 3:
                    emit_fetch(otc + 1)
                ve_z, se_z = sf[("zrdy", par)]
                zbuf = zbp[par]
                for dc in range(4):
                    wpar = dc % 2
                    qeng, psem = (("gpsimd", "s_gp") if wpar == 0 else
                                  ("sync", "s_wb"))
                    OP(qeng, lambda e, s=d_pjc[10 + dc], d=woslab[wpar]:
                       e.dma_start(out=d, in_=s),
                       waits=[sf.get(("wo", wpar), ("s_pe", 0))] +
                             ([cst_done] + entry_waits if (otc == 0 and dc < 2) else []),
                       inc=psem, inc_by=16)
                    ldw = p.cur(psem)
                    for tt in range(ATT):
                        bi = 4 + (dc * ATT + tt) % 2
                        w = [(psem, ldw), ve_z, se_z, psum_free[bi]]
                        for jj in range(H):
                            OP("tensor", lambda e,
                               l=zbuf[:, jj * ATC + tt * 128: jj * ATC + tt * 128 + 128],
                               r=woslab[wpar][:, jj * 512:(jj + 1) * 512],
                               o=PS[bi][:, :], s=(jj == 0):
                               e.matmul(o, l, r, start=s, stop=False),
                               waits=w if jj == 0 else [])
                        OP("tensor", lambda e, o=PS[bi][:, :],
                           r=bo_row[0:1, dc * 512:(dc + 1) * 512]:
                           e.matmul(o, ones1b[0:1, :], r, start=False, stop=True),
                           inc="s_pe")
                        pe_w = p.cur("s_pe")
                        if tt == ATT - 1:
                            sf[("wo", wpar)] = ("s_pe", pe_w)
                        OP("vector", lambda e,
                           xp=xch[:, tt * D + dc * 512: tt * D + (dc + 1) * 512],
                           ps_=PS[bi][:, :]: e.tensor_tensor(xp, xp, ps_, ALU.add),
                           waits=[("s_pe", pe_w),
                                  ("s_x", xld) if tt < AH else ("s_gp", xg1)],
                           inc="s_ve")
                        psum_free[bi] = ("s_ve", p.cur("s_ve"))
                pe_done = ("s_pe", p.cur("s_pe"))
                sf[("ob", par)] = (pe_done,)
                sf[("zb", par)] = pe_done
                ve_all = p.cur("s_ve")
                OP("sync", lambda e, d=dst_view[:, otc * ATT:otc * ATT + AH, :]: e.dma_start(
                    out=d, in_=xch[:, 0:AH * D].rearrange("p (t d) -> p t d", t=AH)),
                   waits=[("s_ve", ve_all)], inc="s_out", inc_by=16)
                OP("gpsimd", lambda e, d=dst_view[:, otc * ATT + AH:(otc + 1) * ATT, :]:
                   e.dma_start(out=d, in_=xch[:, AH * D:ATT * D].rearrange(
                       "p (t d) -> p t d", t=AH)),
                   waits=[("s_ve", ve_all)], inc="s_gp", inc_by=16)
                state["st_gp"] = ("s_gp", p.cur("s_gp"))

        entry = [("s_pe", p.cur("s_pe")), ("s_ve", p.cur("s_ve")), ("s_se", p.cur("s_se"))]
        emit_ffn(v_x, v_xm1f, do_quant=True, entry_waits=entry, ntc=SEQ // FTC)
        entry = [("s_pe", p.cur("s_pe")), ("s_ve", p.cur("s_ve")), ("s_se", p.cur("s_se"))]
        emit_attn_src(v_xm1f, entry)
        entry = [("s_pe", p.cur("s_pe")), ("s_ve", p.cur("s_ve")), ("s_se", p.cur("s_se")),
                 state["o_done"]]
        emit_attn_out(v_xm1f, v_xm2, v_oloc, entry)
        entry = [("s_pe", p.cur("s_pe")), ("s_ve", p.cur("s_ve")), ("s_se", p.cur("s_se"))]
        emit_ffn(v_xm2, v_y, do_quant=False, entry_waits=entry, ntc=2)

        OP("sync", lambda e: e.nop(), waits=[("s_out", p.cur("s_out")),
                                             ("s_cst", p.cur("s_cst")),
                                             ("s_gp", p.cur("s_gp")),
                                             ("s_gp2", p.cur("s_gp2")),
                                             ("s_wa", p.cur("s_wa")),
                                             ("s_wb", p.cur("s_wb"))])

        @block.sync
        def _(eng):
            p.emit(eng, "sync", sems)

        @block.gpsimd
        def _(eng):
            p.emit(eng, "gpsimd", sems)

        @block.tensor
        def _(eng):
            p.emit(eng, "tensor", sems)

        @block.vector
        def _(eng):
            p.emit(eng, "vector", sems)

        @block.scalar
        def _(eng):
            p.emit(eng, "scalar", sems)

    return nc


def _host_prep(inputs):
    x = np.ascontiguousarray(np.asarray(inputs["L0_attn_x"], dtype=np.float32).reshape(4, SEQ, D))
    wqT = np.ascontiguousarray(np.asarray(inputs["L0_attn_Wq"], dtype=np.float32).T)
    wkT = np.ascontiguousarray(np.asarray(inputs["L0_attn_Wk"], dtype=np.float32).T)
    wvT = np.ascontiguousarray(np.asarray(inputs["L0_attn_Wv"], dtype=np.float32).T)
    woT = np.ascontiguousarray(np.asarray(inputs["L0_attn_Wo"], dtype=np.float32).T)
    w1T = np.zeros((D, HID), dtype=np.float32)
    w1T[:, :HID_REAL] = np.asarray(inputs["L0_ffn_w1"], dtype=np.float32).T
    w3T = np.zeros((D, HID), dtype=np.float32)
    w3T[:, :HID_REAL] = np.asarray(inputs["L0_ffn_w3"], dtype=np.float32).T
    w2T = np.zeros((HID, D), dtype=np.float32)
    w2T[:HID_REAL, :] = np.asarray(inputs["L0_ffn_w2"], dtype=np.float32).T
    bq = np.asarray(inputs["L0_attn_bq"], dtype=np.float32)
    bk = np.asarray(inputs["L0_attn_bk"], dtype=np.float32)
    bv = np.asarray(inputs["L0_attn_bv"], dtype=np.float32)
    bo = np.asarray(inputs["L0_attn_bo"], dtype=np.float32)
    bo_row = bo[None, :].astype(ml_dtypes.bfloat16)
    ones1b = np.ones((1, 128), dtype=np.float32).astype(ml_dtypes.bfloat16)
    ident = np.eye(128, dtype=np.float32).astype(ml_dtypes.bfloat16)
    m = np.arange(128)[:, None]
    n = np.arange(256)[None, :]
    maskB = np.where((m % GT) == (n % GT), 0.0, -1e9).astype(np.float32)
    cst = np.tile(np.array([[EPS, -0.5, 1.0 / np.sqrt(128.0), 0.0]], dtype=np.float32), (128, 1))
    ones1 = np.ones((1, 128), dtype=np.float32)
    ones128 = np.ones((128, 1), dtype=np.float32)
    shared = dict(wkT=wkT, wvT=wvT, woT=woT, w1T=w1T, w3T=w3T, w2T=w2T,
                  bo_row=bo_row, ones1b=ones1b, ident=ident, maskB=maskB,
                  cst=cst, ones1=ones1, ones128=ones128)
    in_maps = []
    for c in range(NCORES):
        sq, hf = c // 2, c % 2
        m_ = dict(shared)
        m_["x_in"] = np.ascontiguousarray(x[sq])
        m_["wqTm"] = np.ascontiguousarray(wqT[:, hf * 8 * 128:(hf * 8 + 8) * 128])
        bqkv = np.zeros((128, 40), dtype=np.float32)
        bqkv[:, 0:8] = bq.reshape(16, 128).T[:, hf * 8:(hf + 1) * 8]
        bqkv[:, 8:24] = bk.reshape(16, 128).T
        bqkv[:, 24:40] = bv.reshape(16, 128).T
        m_["bqkv"] = bqkv
        in_maps.append(m_)
    return in_maps


def kernel(**inputs):
    in_maps = _host_prep(inputs)
    nc = build_nc("fused")
    res = run_bass_kernel_spmd(nc, in_maps, core_ids=list(range(NCORES)))
    y = np.stack([np.asarray(r["y_out"]) for r in res.results], axis=0)
    return np.ascontiguousarray(y.reshape(4, SEQ, D)).astype(np.float32)
